# version 36
# baseline (speedup 1.0000x reference)
import numpy as np
import ml_dtypes
import concourse.bass as bass
import concourse.mybir as mybir
from concourse.bass_utils import run_bass_kernel_spmd

F32 = mybir.dt.float32
BF16 = mybir.dt.bfloat16
I16 = mybir.dt.int16
ALU = mybir.AluOpType
AF = mybir.ActivationFunctionType
AX = mybir.AxisListType

C_RND = 1.5 * 2.0 ** 23
N_CORES = 8
D = 1024
KC = 8
NPAIR = 8
M = 64
GRP = 32
DH = 4096
NHC = 32
PINV_ITERS = 6
EPS_LN = 1e-5
BF = ml_dtypes.bfloat16


class Tok:
    __slots__ = ("sem", "val")
    def __init__(self, sem, val):
        self.sem = sem
        self.val = val


class Buf:
    __slots__ = ("ap", "ready", "users", "name")
    def __init__(self, ap, name=""):
        self.ap = ap
        self.ready = []
        self.users = []
        self.name = name

    def view(self, ap):
        return (ap, self)


class Builder:
    def __init__(self, L=2048, nseq=2, dbg=()):
        assert L % 128 == 0
        self.L = L
        self.NB = L // 128
        self.TT = min(512, L)
        self.NT = L // self.TT
        self.nseq = nseq
        self.GRP = L // M
        self.dbg = set(dbg)
        self.nc = bass.Bass("TRN2", target_bir_lowering=False)
        nc = self.nc
        self.esem = {}
        self.ecnt = {}
        for e in ("pe", "dve", "act", "gp"):
            self.esem[e] = nc.alloc_semaphore(name="sem_" + e)
            self.ecnt[e] = 0
        self.dsem = [nc.alloc_semaphore(name="dsem%d" % i) for i in range(16)]
        self.dcnt = [0] * 16
        self.dnext = 0
        self.waited = {e: {} for e in ("pe", "dve", "act", "gp", "sp")}
        self.hosts = {}
        self.ins = {}
        self.outs = []
        self.dbg_bufs = {}

    def sb(self, name, p, f, dt):
        return Buf(self.nc.alloc_sbuf_tensor(name, [p, f], dt).ap(), name)

    def ps(self, name, p, f, dt=F32):
        return Buf(self.nc.alloc_psum_tensor(name, [p, f], dt).ap(), name)

    def dram(self, name, shape, dt, kind="Internal"):
        return self.nc.dram_tensor(name, list(shape), dt, kind=kind).ap()

    def inp(self, name, shape, dt):
        ap = self.dram(name, shape, dt, kind="ExternalInput")
        self.ins[name] = ap
        return ap

    def host_const(self, name, arr):
        dt = {np.dtype(np.float32): F32, np.dtype(BF): BF16}[arr.dtype]
        ap = self.inp(name, arr.shape, dt)
        self.hosts[name] = arr
        return ap

    def _engine(self, e):
        nc = self.nc
        return {"pe": nc.tensor, "dve": nc.vector, "act": nc.scalar,
                "gp": nc.gpsimd, "sp": nc.sync}[e]

    def _wait(self, e, tok):
        if tok is None:
            return
        key = id(tok.sem)
        w = self.waited[e]
        if w.get(key, -1) >= tok.val:
            return
        self._engine(e).wait_ge(tok.sem, tok.val)
        w[key] = tok.val

    @staticmethod
    def _buf(x):
        return x[1] if isinstance(x, tuple) else x

    def op(self, e, emits, reads=(), writes=()):
        if not isinstance(emits, (list, tuple)):
            emits = [emits]
        reads = [self._buf(b) for b in reads]
        writes = [self._buf(b) for b in writes]
        for b in reads:
            for t in b.ready:
                self._wait(e, t)
        for b in writes:
            for t in b.users:
                self._wait(e, t)
            for t in b.ready:
                self._wait(e, t)
        instr = None
        for em in emits:
            instr = em()
        sem = self.esem[e]
        self.ecnt[e] += 1
        instr.then_inc(sem, 1)
        tok = Tok(sem, self.ecnt[e])
        for b in reads:
            b.users.append(tok)
        for b in writes:
            b.ready = [tok]
            b.users = []
        return tok

    def dma(self, out, in_, reads=(), writes=(), q="sp", wait=(), **kw):
        oap = out.ap if isinstance(out, Buf) else out
        iap = in_.ap if isinstance(in_, Buf) else in_
        e = q
        reads = [self._buf(b) for b in reads]
        writes = [self._buf(b) for b in writes]
        if isinstance(in_, Buf):
            reads = reads + [in_]
        if isinstance(out, Buf):
            writes = writes + [out]
        for t in wait:
            self._wait(e, t)
        for b in reads:
            for t in b.ready:
                self._wait(e, t)
        for b in writes:
            for t in b.users:
                self._wait(e, t)
            for t in b.ready:
                self._wait(e, t)
        i = self.dnext
        self.dnext = (self.dnext + 1) % len(self.dsem)
        self._wait(e, Tok(self.dsem[i], self.dcnt[i]))
        instr = self._engine(e).dma_start(out=oap, in_=iap, **kw)
        self.dcnt[i] += 16
        instr.then_inc(self.dsem[i], 16)
        tok = Tok(self.dsem[i], self.dcnt[i])
        for b in reads:
            b.users.append(tok)
        for b in writes:
            b.ready = [tok]
            b.users = []
        return tok

    def barrier(self):
        toks = [Tok(self.esem[e], self.ecnt[e]) for e in self.esem]
        toks += [Tok(self.dsem[i], self.dcnt[i]) for i in range(len(self.dsem))]
        for e in ("pe", "dve", "act", "gp", "sp"):
            for t in toks:
                self._wait(e, t)

    def dbg_out(self, name, shape, dt=F32):
        if name not in self.dbg:
            return None
        ap = self.dram("dbg_" + name, shape, dt, kind="ExternalOutput")
        b = Buf(ap, "dbg_" + name)
        self.dbg_bufs[name] = b
        self.outs.append("dbg_" + name)
        return b

    def build(self):
        nc = self.nc
        L = self.L
        nseq = self.nseq

        self.x_d = self.inp("x", [nseq, L, D], F32)
        self.wk_d = self.inp("Wk", [D, D], F32)
        self.wq_d = self.inp("Wq", [D, D], F32)
        self.wv_d = self.inp("Wv", [D, D], F32)
        self.wo_d = self.inp("Wo", [D, D], F32)
        self.bo_d = self.inp("bo", [1, D], F32)
        self.w1_d = self.inp("w1", [DH, D], F32)
        self.w2_d = self.inp("w2", [D, DH], F32)
        self.cw_d = self.inp("convw_pp", [128, 3 * NHC], F32)
        self.cb_d = self.inp("convb_pp", [128, NHC], F32)
        self.out_d = self.dram("out", [nseq, L, D], F32, kind="ExternalOutput")
        self.outs.append("out")

        idf_h = np.eye(128, dtype=np.float32)
        idb_h = np.eye(128, dtype=BF)
        ideye_h = np.zeros((128, M), dtype=np.float32)
        for p in range(128):
            ideye_h[p, p % M] = 1.0
        masksT_h = np.zeros((2, 128), dtype=np.float32)
        masksT_h[0, :M] = 1.0
        masksT_h[1, M:] = 1.0
        self.idf_d = self.host_const("idf", idf_h)
        self.idb_d = self.host_const("idb", idb_h)
        self.ideye_d = self.host_const("ideye", ideye_h)
        self.masksT_d = self.host_const("masksT", masksT_h)

        self.xqs_sp = self.dram("xqs_sp", [nseq, KC, 128, L], BF16)
        self.amix_sp = self.dram("amix_sp", [nseq, NPAIR, 128, L], BF16)
        self.attn_sp = self.dram("attn_sp", [nseq, L // 128, 128, D], BF16)
        self.q_sp = self.dram("q_sp", [NPAIR, 128, L], BF16)
        self.w2t_sp = self.dram("w2t_sp", [NHC, 128, D], BF16)
        self.wo_sp = self.dram("wo_sp", [KC, 128, D], BF16)
        self.wkqv_sp = self.dram("wkqv_sp", [3, NPAIR, 128, KC, 128], BF16)
        self.invrow_sp = self.dram("invrow_sp", [1, L], F32)
        self.bo_bf_sp = self.dram("bo_bf_sp", [1, D], BF16)
        self.s2r_sp = self.dram("s2r_sp", [1, self.TT], F32)

        self.w1t = self.sb("w1t", 128, KC * DH, BF16)
        self.h2 = self.sb("h2", 128, 16384, F32)
        self.mix = self.sb("mix", 128, 16384, BF16)
        self.scr = self.sb("scr", 128, 4352, F32)
        self.sml = self.sb("sml", 128, 2048, F32)
        self.sml2 = self.sb("sml2", 128, 2048, F32)
        self.invbc = self.sb("invbc", 128, max(2048, L), F32)
        self.cons = self.sb("cons", 128, 1024, F32)
        self.idb_sb = self.sb("idb_sb", 128, 128, BF16)
        self.idf_sb = self.sb("idf_sb", 128, 128, F32)
        self.onesb = self.sb("onesb", 1, 128, BF16)

        self.pb = [self.ps("pb%d" % i, 128, 512, F32) for i in range(8)]

        with nc.Block() as block:
            @block.sync
            def _(sync):
                self.emit_all()
        return nc

    def emit_all(self):
        self.load_consts()
        gp, ga = self.prep_weights(), self.phase_A(0)
        live = {"p": gp, "a": ga}
        while live:
            for key, k in (("p", 2), ("a", 1)):
                g = live.get(key)
                if g is None:
                    continue
                for _ in range(k):
                    try:
                        next(g)
                    except StopIteration:
                        del live[key]
                        break
        self.barrier()
        for s in range(self.nseq):
            if s > 0:
                self.phase_A_run(s)
                self.barrier()
            self.phase_B1(s)
            self.barrier()
            self.phase_B2(s)
            self.barrier()
            self.phase_B3(s)
            self.barrier()
            self.phase_P4(s)
            self.barrier()

    def load_consts(self):
        nc = self.nc
        self.dma(self.idb_sb, self.idb_d)
        self.dma(self.idf_sb, self.idf_d)
        self.op("dve", lambda: nc.vector.memset(self.onesb.ap, 1.0),
                writes=[self.onesb])
        self.ones_col = Buf(self.cons.ap[:, 0:1], "ones")
        self.op("dve", lambda: nc.vector.memset(self.ones_col.ap, 1.0),
                writes=[self.ones_col])
        self.ideye = Buf(self.cons.ap[:, 8:8 + M], "ideye")
        self.dma(self.ideye, self.ideye_d)
        self.i7 = Buf(self.cons.ap[:, 72:72 + M], "i7")
        self.i15 = Buf(self.cons.ap[:, 136:136 + M], "i15")
        self.i13 = Buf(self.cons.ap[:, 200:200 + M], "i13")
        for b, c in ((self.i7, 7.0), (self.i15, 15.0), (self.i13, 13.0)):
            self.op("dve", lambda b=b, c=c: nc.vector.tensor_scalar(
                b.ap, self.ideye.ap, c, None, ALU.mult),
                reads=[self.ideye], writes=[b])
        self.crnd_col = Buf(self.cons.ap[:, 394:395], "crnd")
        self.op("dve", lambda: nc.vector.memset(self.crnd_col.ap, C_RND),
                writes=[self.crnd_col])
        self.epsln_col = Buf(self.cons.ap[:, 395:396], "epsln")
        self.op("dve", lambda: nc.vector.memset(self.epsln_col.ap, EPS_LN),
                writes=[self.epsln_col])
        self.neghalf_col = Buf(self.cons.ap[:, 396:397], "neghalf")
        self.op("dve", lambda: nc.vector.memset(self.neghalf_col.ap, -0.5),
                writes=[self.neghalf_col])
        self.q8_col = Buf(self.cons.ap[:, 397:398], "q8c")
        self.op("dve", lambda: nc.vector.memset(self.q8_col.ap, 0.125),
                writes=[self.q8_col])
        self.masksT = Buf(self.cons.ap[:2, 264:392], "masksT")
        self.dma(self.masksT, self.masksT_d)
        self.masks = Buf(self.cons.ap[:, 392:394], "masks")
        self.op("dve", lambda: nc.vector.memset(self.masks.ap, 0.0),
                writes=[self.masks])
        self.op("dve", lambda: nc.vector.memset(self.cons.ap[:M, 392:393], 1.0),
                reads=[self.masks], writes=[self.masks])
        self.op("dve", lambda: nc.vector.memset(self.cons.ap[M:, 393:394], 1.0),
                reads=[self.masks], writes=[self.masks])

    def prep_weights(self):
        nc = self.nc
        h2 = self.h2
        wtmp = Buf(h2.ap[:, 0:4096], "wtmp")
        wbf = Buf(h2.ap[:, 14336:14848].bitcast(BF16), "wbf")
        wtmpq = [Buf(h2.ap[:, i * 1024:(i + 1) * 1024], "wtq%d" % i)
                 for i in (0, 1)]
        wbfq = [Buf(h2.ap[:, 14336 + i * 512:14848 + i * 512].bitcast(BF16),
                    "wbfq%d" % i) for i in (0, 1)]
        WDS = ((self.wk_d, None), (self.wq_d, self.q8_col),
               (self.wv_d, None), (self.wo_d, None))

        def qkv_ld(j):
            wi, kc = divmod(j, KC)
            self.dma(wtmpq[j % 2], WDS[wi][0][kc * 128:(kc + 1) * 128, :])

        qkv_ld(0)
        for j in range(4 * KC):
            wi, kc = divmod(j, KC)
            if j + 1 < 4 * KC:
                qkv_ld(j + 1)
            wt, wb2 = wtmpq[j % 2], wbfq[j % 2]
            sc = WDS[wi][1]
            if sc is None:
                self.op("act", lambda wt=wt, wb2=wb2: nc.scalar.copy(
                    wb2.ap, wt.ap), reads=[wt], writes=[wb2])
            else:
                self.op("act", lambda wt=wt, wb2=wb2, sc=sc:
                        nc.scalar.activation(wb2.ap, wt.ap, AF.Identity,
                                             scale=sc.ap),
                        reads=[wt, sc], writes=[wb2])
            if wi == 3:
                self.dma(self.wo_sp[kc], wb2.ap, reads=(wb2,), q="act")
            else:
                self.dma(self.wkqv_sp[wi, :, :, kc, :].rearrange(
                    "pr p c -> p pr c"),
                    wb2.ap.rearrange("p (pr c) -> p pr c", pr=NPAIR),
                    reads=(wb2,), q="act")
            yield
        bo_row = Buf(self.sml2.ap[:1, 960:1984], "bo_row")
        self.dma(bo_row, self.bo_d)
        bo_rbf = Buf(self.sml2.ap[:1, 448:960].bitcast(BF16), "bo_rbf")
        self.op("act", lambda: nc.scalar.copy(bo_rbf.ap, bo_row.ap),
                reads=[bo_row], writes=[bo_rbf])
        self.dma(self.bo_bf_sp, bo_rbf.ap, reads=(bo_rbf,))
        self.cw = Buf(self.sml2.ap[:, 0:3 * NHC], "cw")
        self.cb = Buf(self.sml2.ap[:, 96:96 + NHC], "cb")
        self.dma(self.cw, self.cw_d)
        self.dma(self.cb, self.cb_d)
        tern_bufs = dict(
            wtmp=[Buf(h2.ap[:, i * 4096:(i + 1) * 4096], "twtmp%d" % i)
                  for i in (0, 1)],
            ti=Buf(h2.ap[:, 8192:10240].bitcast(I16), "ti"),
            tb=Buf(h2.ap[:, 10240:12288].bitcast(BF16), "tb"),
            stage=Buf(h2.ap[:, 12288:14336].bitcast(BF16), "tstage"),
        )
        yield from self._tern_weight(self.w1_d, DH, D, True, tern_bufs)
        yield from self._tern_weight(self.w2_d, D, DH, False, tern_bufs)

    def _tern_weight(self, wd, nrows, ncol, is_w1, tern_bufs):
        nc = self.nc
        nch = nrows // 128
        wtmps, ti, tb, stage = (tern_bufs[k] for k in
                                ("wtmp", "ti", "tb", "stage"))
        absacc = Buf(self.sml.ap[:, 448:448 + nch], "absacc")
        if is_w1:
            self._absacc = absacc
        else:
            absacc = self._absacc
        def ldw(c):
            self.dma(Buf(wtmps[c % 2].ap[:, 0:ncol], "wtv"),
                     wd[c * 128:(c + 1) * 128, :], writes=(wtmps[c % 2],))

        ldw(0)
        for c in range(nch):
            if c + 1 < nch:
                ldw(c + 1)
            wtmp = wtmps[c % 2]
            self.op("dve", lambda c=c, wtmp=wtmp: nc.vector.reduce_sum(
                absacc.ap[:, c:c + 1], wtmp.ap[:, 0:ncol], axis=AX.X,
                apply_absolute_value=True),
                reads=[wtmp], writes=[absacc])
            yield
        ps = self.pb[0]
        self.op("pe", lambda: nc.tensor.matmul(
            ps.ap[:1, 0:nch], self.ones_col.ap, absacc.ap[:, 0:nch],
            start=True, stop=True),
            reads=[self.ones_col, absacc], writes=[ps])
        tot = Buf(self.sml.ap[:1, 1700:1701], "tot")
        self.op("dve", lambda: nc.vector.reduce_sum(
            tot.ap, ps.ap[:1, 0:nch], axis=AX.X), reads=[ps], writes=[tot])
        mean = Buf(self.sml.ap[:1, 1701:1702], "mean")
        self.op("dve", lambda: nc.vector.tensor_scalar(
            mean.ap, tot.ap, 1.0 / (nrows * ncol), 1e-5, ALU.mult, ALU.max),
            reads=[tot], writes=[mean])
        scale = Buf(self.sml.ap[:1, 1702:1703], "scale")
        self.op("dve", lambda: nc.vector.reciprocal(scale.ap, mean.ap),
                reads=[mean], writes=[scale])
        row2 = Buf(self.sml.ap[:1, 1704:1706], "row2")
        self.op("dve", lambda: nc.vector.tensor_copy(row2.ap[:, 0:1], scale.ap),
                reads=[scale], writes=[row2])
        self.op("dve", lambda: nc.vector.tensor_copy(row2.ap[:, 1:2], mean.ap),
                reads=[mean, row2], writes=[row2])
        ones_row = Buf(self.sml.ap[:1, 1708:1836], "ones_row")
        self.op("dve", lambda: nc.vector.memset(ones_row.ap, 1.0),
                writes=[ones_row])
        ps2 = self.pb[1]
        self.op("pe", lambda: nc.tensor.matmul(
            ps2.ap[:, 0:2], ones_row.ap, row2.ap, start=True, stop=True),
            reads=[ones_row, row2], writes=[ps2])
        if is_w1:
            self.w1_mw = Buf(self.sml2.ap[:, 229:230], "w1_mw")
            self.w1_sc = Buf(self.sml2.ap[:, 230:231], "w1_sc")
            mw_col, sc_col = self.w1_mw, self.w1_sc
        else:
            self.w2_mw = Buf(self.sml2.ap[:, 231:232], "w2_mw")
            self.w2_sc = Buf(self.sml2.ap[:, 232:233], "w2_sc")
            mw_col, sc_col = self.w2_mw, self.w2_sc
        self.op("dve", lambda: nc.vector.tensor_copy(sc_col.ap, ps2.ap[:, 0:1]),
                reads=[ps2], writes=[sc_col])
        self.op("dve", lambda: nc.vector.tensor_copy(mw_col.ap, ps2.ap[:, 1:2]),
                reads=[ps2], writes=[mw_col])
        if is_w1:
            self.cw2 = Buf(self.sml2.ap[:, 128:128 + 3 * NHC], "cw2")
            self.op("dve", lambda: nc.vector.tensor_scalar(
                self.cw2.ap, self.cw.ap, mw_col.ap, None, ALU.mult),
                reads=[self.cw, mw_col], writes=[self.cw2])
        nblk = ncol // 128
        ldw(0)
        for c in range(nch):
            if c + 1 < nch:
                ldw(c + 1)
            wtmp = wtmps[c % 2]
            self.op("act", lambda wtmp=wtmp: nc.scalar.activation(
                wtmp.ap[:, 0:ncol], wtmp.ap[:, 0:ncol], AF.Identity,
                scale=sc_col.ap, bias=self.crnd_col.ap),
                reads=[wtmp, sc_col, self.crnd_col], writes=[wtmp])
            self.op("dve", lambda wtmp=wtmp: nc.vector.tensor_scalar(
                ti.ap[:, 0:ncol], wtmp.ap[:, 0:ncol], C_RND, None,
                ALU.subtract), reads=[wtmp], writes=[ti])
            self.op("dve", lambda: nc.vector.tensor_scalar(
                tb.ap[:, 0:ncol], ti.ap[:, 0:ncol], -1.0, 1.0, ALU.max,
                ALU.min), reads=[ti], writes=[tb])
            for g0 in range(0, nblk, 8):
                gn = min(8, nblk - g0)
                psb = self.pb[2 + (g0 // 8) % 2]
                pbf = psb.ap.bitcast(BF16)
                emits = []
                for j in range(gn):
                    blk = g0 + j
                    emits.append(lambda j=j, blk=blk: nc.tensor.transpose(
                        pbf[:, j * 128:(j + 1) * 128],
                        tb.ap[:, blk * 128:(blk + 1) * 128],
                        self.idb_sb.ap))
                self.op("pe", emits, reads=[tb, self.idb_sb], writes=[psb])
                if is_w1:
                    dst = self.w1t.ap.rearrange("p (kc d) -> p kc d", kc=KC)[
                        :, g0:g0 + gn, c * 128:(c + 1) * 128]
                    self.op("act", lambda dst=dst, gn=gn, pbf=pbf:
                            nc.scalar.copy(
                                dst, pbf[:, 0:gn * 128].rearrange(
                                    "p (kc d) -> p kc d", kc=gn)),
                            reads=[psb], writes=[self.w1t])
                else:
                    self.op("act", lambda g0=g0, gn=gn, pbf=pbf:
                            nc.scalar.copy(
                                stage.ap[:, g0 * 128:(g0 + gn) * 128],
                                pbf[:, 0:gn * 128]),
                            reads=[psb], writes=[stage])
            if not is_w1:
                self.dma(self.w2t_sp[:, :, c * 128:(c + 1) * 128].rearrange(
                    "hc p c2 -> p hc c2"),
                    stage.ap[:, 0:NHC * 128].rearrange(
                        "p (hc c2) -> p hc c2", hc=NHC),
                    reads=(stage,), q="act")
            yield

    def phase_A_run(self, s):
        for _ in self.phase_A(s):
            pass

    def phase_A(self, s):
        nc = self.nc
        L, NB = self.L, self.NB
        scr, sml = self.scr, self.sml
        self.invsx_all = Buf(sml.ap[:, 0:NB], "invsx_all")
        xt = [Buf(scr.ap[:, 0:1024], "xt0"),
              Buf(self.invbc.ap[:, 0:1024], "xt1")]
        xn = [Buf(scr.ap[:, 1024:2048], "xn0"),
              Buf(self.invbc.ap[:, 1024:2048], "xn1")]
        zone2 = [Buf(scr.ap[:, 2048:2560], "zone2_0"),
                 Buf(sml.ap[:, 512:1024], "zone2_1")]
        xqb = [Buf(scr.ap[:, 2560:3072].bitcast(BF16), "xqb0"),
               Buf(sml.ap[:, 1024:1536].bitcast(BF16), "xqb1")]
        xsq_sh = Buf(scr.ap[:, 3072:4096], "xsq_sh")
        rstage = [Buf(sml.ap[:1, 128:256], "rstage0"),
                  Buf(sml.ap[:1, 256:384], "rstage1")]
        stats = [Buf(sml.ap[:, 64:80], "stats0"), Buf(sml.ap[:, 32:48], "st1")]
        CELLS = ("mean", "rsq", "var", "lnv", "rstd", "mb", "m2",
                 "amax", "clip", "rcl", "sx")
        cells = [{n: Buf(stats[p].ap[:, i:i + 1], n + str(p))
                  for i, n in enumerate(CELLS)} for p in (0, 1)]
        xn_fm = Buf(self.mix.ap[:, 0:KC * L], "xn_fm")
        self.xn_fm = xn_fm
        dxn = self.dbg_out("xn_tm", [self.nseq, L, D]) if s == 0 else \
            self.dbg_bufs.get("xn_tm")
        invrow_toks = []

        def load(b):
            self.dma(xt[b % 2], self.x_d[s, b * 128:b * 128 + 128, :])

        def hA(b):
            par = b % 2
            c = cells[par]
            xtb, xnb_ = xt[par], xn[par]
            self.op("dve", lambda: nc.vector.reduce_sum(
                c["mean"].ap, xtb.ap, axis=AX.X), reads=[xtb],
                writes=[c["mean"]])
            self.op("act", lambda: nc.scalar.activation(
                xsq_sh.ap, xtb.ap, AF.Square, accum_out=c["rsq"].ap),
                reads=[xtb], writes=[xsq_sh, c["rsq"]])
            self.op("dve", lambda: nc.vector.tensor_scalar(
                c["m2"].ap, c["mean"].ap, c["mean"].ap, 1.0 / (D * D),
                ALU.mult, ALU.mult), reads=[c["mean"]], writes=[c["m2"]])
            self.op("dve", lambda: nc.vector.tensor_scalar(
                c["var"].ap, c["rsq"].ap, 1.0 / D, c["m2"].ap,
                ALU.mult, ALU.subtract), reads=[c["rsq"], c["m2"]],
                writes=[c["var"]])
            self.op("act", lambda: nc.scalar.activation(
                c["lnv"].ap, c["var"].ap, AF.Ln, bias=self.epsln_col.ap),
                reads=[c["var"], self.epsln_col], writes=[c["lnv"]])
            self.op("act", lambda: nc.scalar.activation(
                c["rstd"].ap, c["lnv"].ap, AF.Exp, scale=self.neghalf_col.ap),
                reads=[c["lnv"], self.neghalf_col], writes=[c["rstd"]])
            self.op("dve", lambda: nc.vector.tensor_scalar(
                c["mb"].ap, c["mean"].ap, c["rstd"].ap, -1.0 / D,
                ALU.mult, ALU.mult), reads=[c["mean"], c["rstd"]],
                writes=[c["mb"]])
            self.op("act", lambda: nc.scalar.activation(
                xnb_.ap, xtb.ap, AF.Identity, scale=c["rstd"].ap,
                bias=c["mb"].ap), reads=[xtb, c["rstd"], c["mb"]],
                writes=[xnb_])
            if dxn is not None:
                self.dma(dxn.ap[s, b * 128:b * 128 + 128, :], xn[par].ap,
                         reads=(xn[par],))

        def hB(b):
            par = b % 2
            c = cells[par]
            xtb, xnb_ = xt[par], xn[par]
            z2 = zone2[par]
            ui_ap = z2.ap.bitcast(I16)
            self.op("dve", lambda: nc.vector.reduce_max(
                c["amax"].ap, xnb_.ap, axis=AX.X, apply_absolute_value=True),
                reads=[xnb_], writes=[c["amax"]])
            self.op("dve", lambda: nc.vector.tensor_scalar(
                c["clip"].ap, c["amax"].ap, 1e-5, None, ALU.max),
                reads=[c["amax"]], writes=[c["clip"]])
            self.op("dve", lambda: nc.vector.reciprocal(
                c["rcl"].ap, c["clip"].ap), reads=[c["clip"]],
                writes=[c["rcl"]])
            self.op("dve", lambda: nc.vector.tensor_scalar(
                c["sx"].ap, c["rcl"].ap, 127.0, None, ALU.mult),
                reads=[c["rcl"]], writes=[c["sx"]])
            self.op("dve", lambda b=b: nc.vector.tensor_scalar(
                self.invsx_all.ap[:, b:b + 1], c["clip"].ap, 1.0 / 127.0,
                None, ALU.mult), reads=[c["clip"]], writes=[self.invsx_all])
            self.op("dve", lambda: nc.vector.tensor_scalar(
                xsq_sh.ap, xnb_.ap, c["sx"].ap, C_RND, ALU.mult, ALU.add),
                reads=[xnb_, c["sx"]], writes=[xsq_sh])
            self.op("dve", lambda ui_ap=ui_ap: nc.vector.tensor_scalar(
                ui_ap, xsq_sh.ap, C_RND, None, ALU.subtract),
                reads=[xsq_sh], writes=[z2])
            self.op("act", lambda ui_ap=ui_ap, par=par: nc.scalar.copy(
                xqb[par].ap, ui_ap), reads=[z2], writes=[xqb[par]])
            self.op("act", lambda par=par: nc.scalar.copy(
                zone2[par].ap.bitcast(BF16), xn[par].ap),
                reads=[xn[par]], writes=[z2])

        def hC(b):
            par = b % 2
            t0 = b * 128
            for idx, src in ((0, zone2[par]), (1, xqb[par])):
                psb = self.pb[4 + 2 * par + idx]
                pbf = psb.ap.bitcast(BF16)
                emits = []
                sap = src.ap.bitcast(BF16) if idx == 0 else src.ap
                for kc in range(KC):
                    emits.append(lambda kc=kc, sap=sap, pbf=pbf:
                                 nc.tensor.transpose(
                                     pbf[:, kc * 128:(kc + 1) * 128],
                                     sap[:, kc * 128:(kc + 1) * 128],
                                     self.idb_sb.ap))
                self.op("pe", emits, reads=[src, self.idb_sb], writes=[psb])
                if idx == 0:
                    dst = xn_fm.ap.rearrange("p (kc t) -> p kc t", kc=KC)[
                        :, :, t0:t0 + 128]
                    self.op("act", lambda dst=dst, pbf=pbf:
                            nc.scalar.copy(
                                dst, pbf.rearrange("p (kc c) -> p kc c",
                                                   kc=KC)),
                            reads=[psb], writes=[xn_fm])
                else:
                    self.op("act", lambda pbf=pbf, par=par: nc.scalar.copy(
                        xqb[par].ap, pbf), reads=[psb], writes=[xqb[par]])
                    self.dma(self.xqs_sp[s, :, :, t0:t0 + 128].rearrange(
                        "kc p c -> p kc c"),
                        xqb[par].ap.rearrange("p (kc c) -> p kc c", kc=KC),
                        reads=(xqb[par],), q="act")
            psr = self.pb[5 + 2 * par]
            self.op("pe", lambda b=b, psr=psr: nc.tensor.transpose(
                psr.ap[:1, 0:128], self.invsx_all.ap[:, b:b + 1],
                self.idf_sb.ap), reads=[self.invsx_all, self.idf_sb],
                writes=[psr])
            self.op("act", lambda psr=psr, par=par: nc.scalar.copy(
                rstage[par].ap, psr.ap[:1, 0:128]), reads=[psr],
                writes=[rstage[par]])
            invrow_toks.append(
                self.dma(self.invrow_sp[:, t0:t0 + 128], rstage[par].ap,
                         reads=(rstage[par],)))

        load(0)
        for b in range(NB):
            if b + 1 < NB:
                load(b + 1)
            hA(b)
            if b >= 1:
                hB(b - 1)
            if b >= 2:
                hC(b - 2)
            yield
        hB(NB - 1)
        hC(NB - 2)
        hC(NB - 1)
        ivb = Buf(self.invbc.ap[:, 0:L], "invbc")
        self.dma(ivb, self.invrow_sp.broadcast_to([128, L]),
                 wait=invrow_toks, writes=(xt[1], xn[1]))
        self.invbc_b = ivb

    def phase_B1(self, s):
        nc = self.nc
        L, NB = self.L, self.NB
        xn_fm = self.xn_fm
        h2 = self.h2
        h2bf = h2.ap.bitcast(BF16)
        kp = [Buf(h2bf[:, i * 2048:i * 2048 + L], "k_p%d" % i) for i in (0, 1)]
        vp = [Buf(h2bf[:, 4096 + i * 2048:4096 + i * 2048 + L], "v_p%d" % i)
              for i in (0, 1)]
        p3 = Buf(h2bf[:, 8192:8192 + L], "p3")
        p3t = Buf(h2bf[:, 10240:10240 + L], "p3t")
        qstage = [Buf(h2bf[:, 12288 + i * 512:12288 + (i + 1) * 512],
                      "qst%d" % i) for i in (0, 1)]
        wpb = [[Buf(h2bf[:, base + i * 1024:base + (i + 1) * 1024],
                    "wp%d_%d" % (g, i)) for i in range(3)]
               for g, base in ((0, 13312), (1, 18432))]
        h2f = self.h2.ap
        self.s2bd_all = Buf(h2f[:, 8192:9216], "s2bd_all")
        lam3 = Buf(self.scr.ap[:, 0:L], "lam3")
        xl = Buf(self.scr.ap[:, 2048:2560], "xl")
        xlb = Buf(self.scr.ap[:, 2560:2816].bitcast(BF16), "xlb")
        s2e = Buf(self.scr.ap[:, 2816:2880], "s2e")
        self.O_all = Buf(self.sml2.ap[:, 256:768], "O_all")
        self.s2n_all = Buf(self.sml2.ap[:, 768:1280], "s2n")
        self.s2T_all = Buf(self.sml2.ap[:, 1280:1792], "s2T")
        qlkl = self.sml2.ap[:, 1792:2048].bitcast(BF16)
        self.ql_all = Buf(qlkl[:, 0:NPAIR * M], "ql_all")
        kl_region = self.sml.ap[:, 1792:2048].bitcast(BF16)
        self.kl_all = Buf(kl_region[:, 0:NPAIR * M], "kl_all")
        rs3 = [Buf(self.sml.ap[:, 80 + i:81 + i], "rs3_%d" % i) for i in (0, 1)]
        rmax3 = Buf(self.sml.ap[:, 82:83], "rmax3")
        nmax3 = Buf(self.sml.ap[:, 83:84], "nmax3")
        rc3 = [Buf(self.sml.ap[:, 84 + i:85 + i], "rc3_%d" % i) for i in (0, 1)]
        s2m = Buf(self.sml.ap[:, 96:97], "s2m")
        s2nm = Buf(self.sml.ap[:, 97:98], "s2nm")
        s2s = Buf(self.sml.ap[:, 98:99], "s2s")
        s2r = Buf(self.sml.ap[:, 99:100], "s2r")
        bd = Buf(self.sml.ap[:, 256:320].bitcast(BF16), "bd")
        self.op("dve", lambda: nc.vector.memset(bd.ap, 0.0), writes=[bd])

        for kc in range(KC):
            self.op("dve", lambda kc=kc: nc.vector.reduce_sum(
                xl.ap[:, kc * M:(kc + 1) * M],
                xn_fm.ap[:, kc * L:(kc + 1) * L].rearrange(
                    "p (g w) -> p g w", w=self.GRP), axis=AX.X),
                reads=[xn_fm], writes=[xl])
        self.op("dve", lambda: nc.vector.tensor_scalar(
            xlb.ap, xl.ap, 1.0 / self.GRP, None, ALU.mult),
            reads=[xl], writes=[xlb])

        ds2 = self.dbg_out("s2n", [self.nseq, 128, 512]) if s == 0 else \
            self.dbg_bufs.get("s2n")
        dO = self.dbg_out("O_all", [self.nseq, 128, 512]) if s == 0 else \
            self.dbg_bufs.get("O_all")

        def tail(pr):
            par = pr % 2
            for g in range(2):
                psb = self.pb[4 + g]
                pbf = psb.ap.bitcast(BF16)
                emits = []
                for j in range(8):
                    tb = g * 8 + j
                    if tb >= NB:
                        break
                    emits.append(lambda j=j, tb=tb, pbf=pbf:
                                 nc.tensor.transpose(
                                     pbf[:, j * 128:(j + 1) * 128],
                                     p3.ap[:, tb * 128:tb * 128 + 128],
                                     self.idb_sb.ap))
                if not emits:
                    break
                gn = len(emits)
                self.op("pe", emits, reads=[p3, self.idb_sb], writes=[psb])
                self.op("dve", lambda g=g, gn=gn, pbf=pbf:
                        nc.vector.tensor_copy(
                            p3t.ap[:, g * 1024:g * 1024 + gn * 128],
                            pbf[:, 0:gn * 128]),
                        reads=[psb], writes=[p3t])
            psO = self.pb[6]
            emits = []
            for tb in range(NB):
                emits.append(lambda tb=tb, psO=psO, par=par: nc.tensor.matmul(
                    psO.ap[:, 0:128], p3t.ap[:, tb * 128:tb * 128 + 128],
                    vp[par].ap[:, tb * 128:tb * 128 + 128],
                    start=(tb == 0), stop=(tb == NB - 1)))
            self.op("pe", emits, reads=[p3t, vp[par]], writes=[psO])
            self.op("dve", lambda pr=pr, par=par: nc.vector.tensor_scalar(
                self.O_all.ap[0:64, pr * M:(pr + 1) * M], psO.ap[0:64, 0:64],
                rc3[par].ap[0:64, :], None, ALU.mult),
                reads=[psO, rc3[par]], writes=[self.O_all])
            self.op("dve", lambda pr=pr, par=par: nc.vector.tensor_scalar(
                self.O_all.ap[64:128, pr * M:(pr + 1) * M],
                psO.ap[64:128, 64:128], rc3[par].ap[64:128, :], None,
                ALU.mult), reads=[psO, rc3[par]], writes=[self.O_all])

        def wload(pr):
            ws = wpb[pr % 2]
            for wi, wb in ((1, ws[0]), (0, ws[1]), (2, ws[2])):
                self.dma(Buf(wb.ap, wb.name),
                         self.wkqv_sp[wi, pr].rearrange("p kc c -> p (kc c)"),
                         writes=(wb,), q="act" if wi != 0 else "sp")

        wload(0)
        for pr in range(NPAIR):
            par = pr % 2
            wq_b, wk_b, wv_b = wpb[par]
            if pr + 1 < NPAIR:
                wload(pr + 1)
            for (wb, dst) in ((wq_b, None), (wk_b, kp[par])):
                for n4 in range(max(1, L // 512)):
                    w512 = min(512, L)
                    psq = self.pb[n4 % 2]
                    emits = []
                    for kc in range(KC):
                        emits.append(lambda kc=kc, n4=n4, psq=psq, wb=wb,
                                     w512=w512: nc.tensor.matmul(
                                         psq.ap[:, 0:w512],
                                         wb.ap[:, kc * 128:(kc + 1) * 128],
                                         xn_fm.ap[:, kc * L + n4 * 512:
                                                  kc * L + n4 * 512 + w512],
                                         start=(kc == 0), stop=(kc == KC - 1)))
                    self.op("pe", emits, reads=[wb, xn_fm], writes=[psq])
                    if dst is None:
                        qs = qstage[n4 % 2]
                        self.op("dve", lambda n4=n4, psq=psq, qs=qs, w512=w512:
                                nc.vector.tensor_copy(
                                    qs.ap[:, 0:w512], psq.ap[:, 0:w512]),
                                reads=[psq], writes=[qs])
                        self.dma(self.q_sp[pr][:, n4 * 512:n4 * 512 + w512],
                                 qs.ap[:, 0:w512], reads=(qs,), q="act")
                    else:
                        self.op("dve", lambda n4=n4, psq=psq, dst=dst,
                                w512=w512: nc.vector.tensor_copy(
                                    dst.ap[:, n4 * 512:n4 * 512 + w512],
                                    psq.ap[:, 0:w512]),
                                reads=[psq], writes=[dst])
            for g4 in range((NB + 3) // 4):
                psv = self.pb[2 + g4 % 2]
                emits = []
                gn = min(4, NB - g4 * 4)
                for j in range(gn):
                    tb = g4 * 4 + j
                    for kc in range(KC):
                        emits.append(lambda kc=kc, tb=tb, j=j, psv=psv,
                                     wv_b=wv_b: nc.tensor.matmul(
                                         psv.ap[:, j * 128:j * 128 + 128],
                                         xn_fm.ap[:, kc * L + tb * 128:
                                                  kc * L + tb * 128 + 128],
                                         wv_b.ap[:, kc * 128:(kc + 1) * 128],
                                         start=(kc == 0), stop=(kc == KC - 1)))
                self.op("pe", emits, reads=[wv_b, xn_fm], writes=[psv])
                self.op("act", lambda g4=g4, gn=gn, psv=psv, par=par:
                        nc.scalar.copy(
                            vp[par].ap[:, g4 * 512:g4 * 512 + gn * 128],
                            psv.ap[:, 0:gn * 128]),
                        reads=[psv], writes=[vp[par]])
            for (wb, dstall) in ((wq_b, self.ql_all), (wk_b, self.kl_all)):
                psl = self.pb[4]
                emits = []
                for kc in range(KC):
                    emits.append(lambda kc=kc, psl=psl, wb=wb:
                                 nc.tensor.matmul(
                                     psl.ap[:, 0:M],
                                     wb.ap[:, kc * 128:(kc + 1) * 128],
                                     xlb.ap[:, kc * M:(kc + 1) * M],
                                     start=(kc == 0), stop=(kc == KC - 1)))
                self.op("pe", emits, reads=[wb, xlb], writes=[psl])
                self.op("dve", lambda psl=psl, dstall=dstall, pr=pr:
                        nc.vector.tensor_copy(
                            dstall.ap[:, pr * M:(pr + 1) * M], psl.ap[:, 0:M]),
                        reads=[psl], writes=[dstall])
            self.op("dve", lambda pr=pr: nc.vector.tensor_copy(
                bd.ap[0:64, 0:64], self.ql_all.ap[0:64, pr * M:(pr + 1) * M]),
                reads=[self.ql_all, bd], writes=[bd])
            self.op("dve", lambda pr=pr: nc.vector.tensor_copy(
                bd.ap[64:128, 64:128],
                self.ql_all.ap[64:128, pr * M:(pr + 1) * M]),
                reads=[self.ql_all, bd], writes=[bd])
            ps2l = self.pb[5]
            self.op("pe", lambda pr=pr: nc.tensor.matmul(
                ps2l.ap[:, 0:M], bd.ap, self.kl_all.ap[:, pr * M:(pr + 1) * M],
                start=True, stop=True),
                reads=[bd, self.kl_all], writes=[ps2l])
            self.op("dve", lambda: nc.vector.reduce_max(
                s2m.ap, ps2l.ap[:, 0:M], axis=AX.X), reads=[ps2l], writes=[s2m])
            self.op("dve", lambda: nc.vector.tensor_scalar(
                s2nm.ap, s2m.ap, -1.0, None, ALU.mult), reads=[s2m],
                writes=[s2nm])
            self.op("act", lambda: nc.scalar.activation(
                s2e.ap, ps2l.ap[:, 0:M], AF.Exp, bias=s2nm.ap,
                accum_out=s2s.ap), reads=[ps2l, s2nm], writes=[s2e, s2s])
            self.op("dve", lambda: nc.vector.reciprocal(s2r.ap, s2s.ap),
                    reads=[s2s], writes=[s2r])
            self.op("dve", lambda pr=pr: nc.vector.tensor_scalar(
                self.s2n_all.ap[:, pr * M:(pr + 1) * M], s2e.ap, s2r.ap, None,
                ALU.mult), reads=[s2e, s2r], writes=[self.s2n_all])
            for n4 in range(max(1, L // 512)):
                w512 = min(512, L)
                psq = self.pb[n4 % 2]
                self.op("pe", lambda psq=psq, n4=n4, w512=w512, par=par:
                        nc.tensor.matmul(
                            psq.ap[:, 0:w512], bd.ap,
                            kp[par].ap[:, n4 * 512:n4 * 512 + w512],
                            start=True, stop=True),
                        reads=[bd, kp[par]], writes=[psq])
                self.op("act", lambda psq=psq, n4=n4, w512=w512:
                        nc.scalar.copy(
                            lam3.ap[:, n4 * 512:n4 * 512 + w512],
                            psq.ap[:, 0:w512]),
                        reads=[psq], writes=[lam3])
            if pr > 0:
                tail(pr - 1)
            self.op("dve", lambda: nc.vector.reduce_max(
                rmax3.ap, lam3.ap, axis=AX.X), reads=[lam3], writes=[rmax3])
            self.op("dve", lambda: nc.vector.tensor_scalar(
                nmax3.ap, rmax3.ap, -1.0, None, ALU.mult),
                reads=[rmax3], writes=[nmax3])
            self.op("act", lambda par=par: nc.scalar.activation(
                p3.ap, lam3.ap, AF.Exp, bias=nmax3.ap,
                accum_out=rs3[par].ap),
                reads=[lam3, nmax3], writes=[p3, rs3[par]])
            self.op("dve", lambda par=par: nc.vector.reciprocal(
                rc3[par].ap, rs3[par].ap), reads=[rs3[par]], writes=[rc3[par]])
        tail(NPAIR - 1)
        bdp = Buf(h2f[:, 9216:10240], "bd_pre")
        self.op("dve", lambda: nc.vector.memset(bdp.ap, 0.0), writes=[bdp])
        sv0 = self.s2n_all.ap[0:64, :].rearrange("p (pr m) -> p pr m", pr=NPAIR)
        dv0 = bdp.ap[0:64, :].rearrange("p (pr c) -> p pr c", pr=NPAIR)[
            :, :, 0:M]
        sv1 = self.s2n_all.ap[64:128, :].rearrange("p (pr m) -> p pr m",
                                                   pr=NPAIR)
        dv1 = bdp.ap[64:128, :].rearrange("p (pr c) -> p pr c", pr=NPAIR)[
            :, :, M:128]
        self.op("dve", lambda: nc.vector.tensor_copy(dv0, sv0),
                reads=[self.s2n_all, bdp], writes=[bdp])
        self.op("dve", lambda: nc.vector.tensor_copy(dv1, sv1),
                reads=[self.s2n_all, bdp], writes=[bdp])
        for g in range(2):
            pst = self.pb[g]
            emits = []
            for j in range(4):
                pr = g * 4 + j
                emits.append(lambda j=j, pr=pr, pst=pst: nc.tensor.transpose(
                    pst.ap[:, j * 128:(j + 1) * 128],
                    bdp.ap[:, pr * 128:(pr + 1) * 128], self.idf_sb.ap))
            self.op("pe", emits, reads=[bdp, self.idf_sb], writes=[pst])
            self.op("dve", lambda g=g, pst=pst: nc.vector.tensor_copy(
                self.s2bd_all.ap[:, g * 512:(g + 1) * 512], pst.ap),
                reads=[pst], writes=[self.s2bd_all])
        bdv = self.s2bd_all.ap
        self.op("dve", lambda: nc.vector.tensor_copy(
            self.s2T_all.ap[0:64, :].rearrange("p (pr m) -> p pr m", pr=NPAIR),
            bdv[0:64, :].rearrange("p (pr c) -> p pr c", pr=NPAIR)[:, :, 0:M]),
            reads=[self.s2bd_all], writes=[self.s2T_all])
        self.op("dve", lambda: nc.vector.tensor_copy(
            self.s2T_all.ap[64:128, :].rearrange("p (pr m) -> p pr m",
                                                 pr=NPAIR),
            bdv[64:128, :].rearrange("p (pr c) -> p pr c", pr=NPAIR)[
                :, :, M:128]),
            reads=[self.s2T_all, self.s2bd_all], writes=[self.s2T_all])
        if ds2 is not None:
            self.dma(ds2.ap[s], self.s2n_all.ap, reads=(self.s2n_all,))
        if dO is not None:
            self.dma(dO.ap[s], self.O_all.ap, reads=(self.O_all,))

    def phase_B2(self, s):
        nc = self.nc
        W = NPAIR * M
        scr = self.h2
        z = Buf(scr.ap[:, 0:W], "z")
        zT = Buf(scr.ap[:, 512:512 + W], "zT")
        xz = Buf(scr.ap[:, 1024:1024 + W], "xz")
        xzT = Buf(scr.ap[:, 1536:1536 + W], "xzT")
        t1 = Buf(scr.ap[:, 2048:2048 + W], "t1")
        t5 = Buf(scr.ap[:, 2560:2560 + W], "t5")
        s2bd = self.s2bd_all
        xzbd = Buf(scr.ap[:, 4096:4096 + 1024], "xzbd")
        aux = Buf(scr.ap[:, 5120:5120 + 1024], "auxbd")
        zTbd = Buf(scr.ap[:, 3072:3072 + 1024], "zTbd")
        rctile = Buf(scr.ap[:, 6144:6144 + W], "rctile")
        for b in (xzbd, aux, zTbd):
            self.op("dve", lambda b=b: nc.vector.memset(b.ap, 0.0), writes=[b])

        def bdify(dst, src):
            sv0 = src.ap[0:64, :].rearrange("p (pr m) -> p pr m", pr=NPAIR)
            dv0 = dst.ap[0:64, :].rearrange("p (pr c) -> p pr c", pr=NPAIR)[
                :, :, 0:M]
            sv1 = src.ap[64:128, :].rearrange("p (pr m) -> p pr m", pr=NPAIR)
            dv1 = dst.ap[64:128, :].rearrange("p (pr c) -> p pr c", pr=NPAIR)[
                :, :, M:128]
            self.op("dve", lambda: nc.vector.tensor_copy(dv0, sv0),
                    reads=[src, dst], writes=[dst])
            self.op("dve", lambda: nc.vector.tensor_copy(dv1, sv1),
                    reads=[src, dst], writes=[dst])

        i7b = self.i7.ap[:, None, :].broadcast_to([128, NPAIR, M])
        i15b = self.i15.ap[:, None, :].broadcast_to([128, NPAIR, M])
        i13b = self.i13.ap[:, None, :].broadcast_to([128, NPAIR, M])

        def v3(b):
            return b.ap.rearrange("p (pr m) -> p pr m", pr=NPAIR)

        psc = self.pb[0]
        self.op("pe", lambda: nc.tensor.matmul(
            psc.ap[:2, 0:W], self.masks.ap, self.s2n_all.ap, start=True,
            stop=True), reads=[self.masks, self.s2n_all], writes=[psc])
        csum = Buf(self.sml.ap[:2, 108:108 + W], "csum")
        self.op("dve", lambda: nc.vector.tensor_copy(csum.ap, psc.ap[:2, 0:W]),
                reads=[psc], writes=[csum])
        cmax = Buf(self.sml.ap[:2, 620:628], "cmax")
        self.op("dve", lambda: nc.vector.reduce_max(
            cmax.ap, csum.ap.rearrange("p (pr m) -> p pr m", pr=NPAIR),
            axis=AX.X), reads=[csum], writes=[cmax])
        psr = self.pb[1]
        self.op("pe", lambda: nc.tensor.matmul(
            psr.ap[:2, 0:W], self.masks.ap, self.s2T_all.ap, start=True,
            stop=True), reads=[self.masks, self.s2T_all], writes=[psr])
        rsum = Buf(self.sml.ap[:2, 640:640 + W], "rsum")
        self.op("dve", lambda: nc.vector.tensor_copy(rsum.ap, psr.ap[:2, 0:W]),
                reads=[psr], writes=[rsum])
        rmax = Buf(self.sml.ap[:2, 1160:1168], "rmax")
        self.op("dve", lambda: nc.vector.reduce_max(
            rmax.ap, rsum.ap.rearrange("p (pr m) -> p pr m", pr=NPAIR),
            axis=AX.X), reads=[rsum], writes=[rmax])
        den = Buf(self.sml.ap[:2, 1170:1178], "den")
        self.op("dve", lambda: nc.vector.tensor_tensor(
            den.ap, cmax.ap, rmax.ap, ALU.mult), reads=[cmax, rmax],
            writes=[den])
        rden = Buf(self.sml.ap[:2, 1180:1188], "rden")
        self.op("dve", lambda: nc.vector.reciprocal(rden.ap, den.ap),
                reads=[den], writes=[rden])
        rdex = Buf(self.sml.ap[:2, 1190:1190 + W], "rdex")
        self.op("dve", lambda: nc.vector.tensor_copy(
            rdex.ap.rearrange("p (pr m) -> p pr m", pr=NPAIR),
            rden.ap[:, :, None].broadcast_to([2, NPAIR, M])),
            reads=[rden], writes=[rdex])
        psb2 = self.pb[2]
        self.op("pe", lambda: nc.tensor.matmul(
            psb2.ap[:, 0:W], self.masksT.ap, rdex.ap, start=True, stop=True),
            reads=[self.masksT, rdex], writes=[psb2])
        self.op("dve", lambda: nc.vector.tensor_copy(rctile.ap, psb2.ap[:, 0:W]),
                reads=[psb2], writes=[rctile])
        self.op("dve", lambda: nc.vector.tensor_tensor(
            z.ap, self.s2T_all.ap, rctile.ap, ALU.mult),
            reads=[self.s2T_all, rctile], writes=[z])
        self.op("dve", lambda: nc.vector.tensor_tensor(
            zT.ap, self.s2n_all.ap, rctile.ap, ALU.mult),
            reads=[self.s2n_all, rctile], writes=[zT])

        def pair_mms(psd, lhsbd, rhs):
            emits = []
            for pr in range(NPAIR):
                emits.append(lambda pr=pr: nc.tensor.matmul(
                    psd.ap[:, pr * M:(pr + 1) * M],
                    lhsbd.ap[:, pr * 128:(pr + 1) * 128],
                    rhs.ap[:, pr * M:(pr + 1) * M], start=True, stop=True))
            return emits

        for it in range(PINV_ITERS):
            bdify(zTbd, zT)
            psx = self.pb[3]
            self.op("pe", pair_mms(psx, s2bd, z), reads=[s2bd, z], writes=[psx])
            bdify(aux, psx)
            for g in range(2):
                psxt = self.pb[4 + g]
                emits = []
                for j in range(4):
                    pr = g * 4 + j
                    emits.append(lambda j=j, pr=pr, psxt=psxt:
                                 nc.tensor.transpose(
                                     psxt.ap[:, j * 128:(j + 1) * 128],
                                     aux.ap[:, pr * 128:(pr + 1) * 128],
                                     self.idf_sb.ap))
                self.op("pe", emits, reads=[aux, self.idf_sb], writes=[psxt])
                self.op("dve", lambda g=g, psxt=psxt: nc.vector.tensor_copy(
                    xzbd.ap[:, g * 512:(g + 1) * 512], psxt.ap),
                    reads=[psxt], writes=[xzbd])
            self.op("dve", lambda: nc.vector.scalar_tensor_tensor(
                v3(t1), psx.ap.rearrange("p (pr m) -> p pr m", pr=NPAIR),
                -1.0, i7b, ALU.mult, ALU.add),
                reads=[psx, self.i7], writes=[t1])
            pst2 = self.pb[5]
            self.op("pe", pair_mms(pst2, xzbd, t1), reads=[xzbd, t1],
                    writes=[pst2])
            self.op("dve", lambda: nc.vector.scalar_tensor_tensor(
                v3(t1), pst2.ap.rearrange("p (pr m) -> p pr m", pr=NPAIR),
                -1.0, i15b, ALU.mult, ALU.add),
                reads=[pst2, self.i15], writes=[t1])
            pst4 = self.pb[6]
            self.op("pe", pair_mms(pst4, xzbd, t1), reads=[xzbd, t1],
                    writes=[pst4])
            self.op("dve", lambda: nc.vector.scalar_tensor_tensor(
                v3(t5), pst4.ap.rearrange("p (pr m) -> p pr m", pr=NPAIR),
                -1.0, i13b, ALU.mult, ALU.add),
                reads=[pst4, self.i13], writes=[t5])
            bdify(aux, t5)
            psz = self.pb[3]
            self.op("pe", pair_mms(psz, zTbd, t5), reads=[zTbd, t5],
                    writes=[psz])
            pszT = self.pb[5]
            self.op("pe", pair_mms(pszT, aux, zT), reads=[aux, zT],
                    writes=[pszT])
            self.op("dve", lambda: nc.vector.tensor_scalar(
                z.ap, psz.ap, 0.25, None, ALU.mult), reads=[psz], writes=[z])
            self.op("dve", lambda: nc.vector.tensor_scalar(
                zT.ap, pszT.ap, 0.25, None, ALU.mult), reads=[pszT],
                writes=[zT])
        bdify(zTbd, zT)
        pskv = self.pb[0]
        self.op("pe", pair_mms(pskv, zTbd, self.O_all),
                reads=[zTbd, self.O_all], writes=[pskv])
        zkv_region = self.sml.ap[:, 100:1124]
        self.zkv_all = Buf(self.sml2.ap[:, 0:256].bitcast(BF16), "zkv_all")
        self.zkv_all = Buf(self.sml.ap[:, 1280:1536].bitcast(BF16), "zkv_all")
        self.op("dve", lambda: nc.vector.tensor_copy(
            self.zkv_all.ap, pskv.ap), reads=[pskv], writes=[self.zkv_all])
        dz = self.dbg_out("z_all", [self.nseq, 128, 512]) if s == 0 else \
            self.dbg_bufs.get("z_all")
        if dz is not None:
            self.dma(dz.ap[s], z.ap, reads=(z,))

    def phase_B3(self, s):
        nc = self.nc
        L, NB = self.L, self.NB
        h2bf = self.h2.ap.bitcast(BF16)
        q_all = Buf(h2bf[:, 0:NPAIR * L], "q_all")
        amix = Buf(self.mix.ap[:, 0:NPAIR * L], "amix")
        scrf = self.scr
        klbd = Buf(scrf.ap[:, 512:1024].bitcast(BF16), "klbd")
        zkbd = Buf(scrf.ap[:, 1024:1536].bitcast(BF16), "zkbd")
        for (dst, src) in ((klbd, self.kl_all), (zkbd, self.zkv_all)):
            self.op("dve", lambda dst=dst: nc.vector.memset(dst.ap, 0.0),
                    writes=[dst])
            sv0 = src.ap[0:64, :].rearrange("p (pr m) -> p pr m", pr=NPAIR)
            dv0 = dst.ap[0:64, :].rearrange("p (pr c) -> p pr c", pr=NPAIR)[
                :, :, 0:M]
            sv1 = src.ap[64:128, :].rearrange("p (pr m) -> p pr m", pr=NPAIR)
            dv1 = dst.ap[64:128, :].rearrange("p (pr c) -> p pr c", pr=NPAIR)[
                :, :, M:128]
            self.op("dve", lambda dv0=dv0, sv0=sv0: nc.vector.tensor_copy(
                dv0, sv0), reads=[src, dst], writes=[dst])
            self.op("dve", lambda dv1=dv1, sv1=sv1: nc.vector.tensor_copy(
                dv1, sv1), reads=[src, dst], writes=[dst])
        p1 = [Buf(scrf.ap[:, i * 512:(i + 1) * 512].bitcast(BF16), "p1_%d" % i)
              for i in (3, 4)]
        s1t = [Buf(scrf.ap[:, i * 512:(i + 1) * 512].bitcast(BF16), "s1t%d" % i)
               for i in (5, 6)]
        rmax1 = [Buf(self.sml.ap[:, 84 + i:85 + i], "rm%d" % i) for i in (0, 1)]
        nmax1 = [Buf(self.sml.ap[:, 86 + i:87 + i], "nm%d" % i) for i in (0, 1)]
        rm_bl = [Buf(self.sml.ap[:, 88 + i:89 + i], "rmb%d" % i) for i in (0, 1)]
        nm_bl = [Buf(self.sml.ap[:, 90 + i:91 + i], "nmb%d" % i) for i in (0, 1)]
        ssum = [Buf(self.sml.ap[:, 92 + 16 * i:108 + 16 * i], "ss%d" % i)
                for i in (0, 1)]
        srec = [Buf(self.sml.ap[:, 124 + 16 * i:140 + 16 * i], "sr%d" % i)
                for i in (0, 1)]
        damix = self.dbg_out("amix", [self.nseq, NPAIR, 128, L]) if s == 0 \
            else self.dbg_bufs.get("amix")

        for pr in range(NPAIR):
            self.dma(Buf(q_all.ap[:, pr * L:(pr + 1) * L], "qld%d" % pr),
                     self.q_sp[pr], writes=(q_all,),
                     q="act" if pr % 2 else "sp")
        for tb in range(NB):
            par = tb % 2
            psl = self.pb[0 + par]
            psla = self.pb[0 + 2 * par]
            pslb = self.pb[1 + 2 * par]
            emits_a, emits_b = [], []
            for pr in range(NPAIR):
                tgt = psla if pr < 4 else pslb
                lst = emits_a if pr < 4 else emits_b
                lst.append(lambda pr=pr, tb=tb, tgt=tgt: nc.tensor.matmul(
                    tgt.ap[:, (pr % 4) * 128:(pr % 4) * 128 + 128],
                    q_all.ap[:, pr * L + tb * 128:pr * L + tb * 128 + 128],
                    klbd.ap[:, pr * 128:(pr + 1) * 128],
                    start=True, stop=True))
            self.op("pe", emits_a, reads=[q_all, klbd], writes=[psla])
            self.op("pe", emits_b, reads=[q_all, klbd], writes=[pslb])
            self.op("dve", lambda par=par, psla=psla: nc.vector.reduce_max(
                rmax1[par].ap, psla.ap, axis=AX.X),
                reads=[psla], writes=[rmax1[par]])
            self.op("dve", lambda par=par, pslb=pslb: nc.vector.tensor_scalar(
                nmax1[par].ap, rmax1[par].ap, -1.0, None, ALU.mult),
                reads=[rmax1[par]], writes=[nmax1[par]])
            rm_b = rm_bl[par]
            nm_b = nm_bl[par]
            self.op("dve", lambda pslb=pslb, rm_b=rm_b: nc.vector.reduce_max(
                rm_b.ap, pslb.ap, axis=AX.X), reads=[pslb], writes=[rm_b])
            self.op("dve", lambda rm_b=rm_b, nm_b=nm_b: nc.vector.tensor_scalar(
                nm_b.ap, rm_b.ap, -1.0, None, ALU.mult),
                reads=[rm_b], writes=[nm_b])
            self.op("act", lambda par=par, psla=psla: nc.scalar.activation(
                p1[par].ap[:, 0:512], psla.ap, AF.Exp, bias=nmax1[par].ap),
                reads=[psla, nmax1[par]], writes=[p1[par]])
            self.op("act", lambda par=par, pslb=pslb, nm_b=nm_b:
                    nc.scalar.activation(
                        p1[par].ap[:, 512:1024], pslb.ap, AF.Exp,
                        bias=nm_b.ap),
                    reads=[pslb, nm_b], writes=[p1[par]])
            self.op("dve", lambda par=par: nc.vector.reduce_sum(
                ssum[par].ap, p1[par].ap.rearrange(
                    "p (g m) -> p g m", m=M), axis=AX.X),
                reads=[p1[par]], writes=[ssum[par]])
            self.op("dve", lambda par=par: nc.vector.reciprocal(
                srec[par].ap, ssum[par].ap),
                reads=[ssum[par]], writes=[srec[par]])
            self.op("dve", lambda par=par: nc.vector.tensor_tensor(
                p1[par].ap.rearrange("p (g m) -> p g m", m=M),
                p1[par].ap.rearrange("p (g m) -> p g m", m=M),
                srec[par].ap[:, :, None].broadcast_to([128, 16, M]),
                ALU.mult), reads=[p1[par], srec[par]], writes=[p1[par]])
            pstr = self.pb[4 + par]
            pbf = pstr.ap.bitcast(BF16)
            emits = []
            for j in range(4):
                emits.append(lambda j=j, par=par, pbf=pbf: nc.tensor.transpose(
                    pbf[:, j * 128:(j + 1) * 128],
                    p1[par].ap[:, j * 128:(j + 1) * 128], self.idb_sb.ap))
            self.op("pe", emits, reads=[p1[par], self.idb_sb], writes=[pstr])
            self.op("dve", lambda par=par, pbf=pbf: nc.vector.tensor_copy(
                s1t[par].ap[:, 0:512], pbf[:, 0:512]),
                reads=[pstr], writes=[s1t[par]])
            emits = []
            for j in range(4, 8):
                emits.append(lambda j=j, par=par, pbf=pbf: nc.tensor.transpose(
                    pbf[:, (j - 4) * 128:(j - 3) * 128],
                    p1[par].ap[:, j * 128:(j + 1) * 128], self.idb_sb.ap))
            self.op("pe", emits, reads=[p1[par], self.idb_sb], writes=[pstr])
            self.op("dve", lambda par=par, pbf=pbf: nc.vector.tensor_copy(
                s1t[par].ap[:, 512:1024], pbf[:, 0:512]),
                reads=[pstr], writes=[s1t[par]])
            psm = self.pb[6 + par]
            emits = []
            for pr in range(NPAIR):
                emits.append(lambda pr=pr, par=par, psm=psm: nc.tensor.matmul(
                    psm.ap[:, (pr % 4) * 128:(pr % 4) * 128 + 128]
                    if pr < 4 else
                    psm.ap[:, (pr % 4) * 128:(pr % 4) * 128 + 128],
                    zkbd.ap[:, pr * 128:(pr + 1) * 128],
                    s1t[par].ap[:, pr * 128:pr * 128 + 128],
                    start=True, stop=True))
            self.op("pe", emits[:4], reads=[zkbd, s1t[par]], writes=[psm])
            for j4 in range(4):
                self.op("act", lambda j4=j4, tb=tb, psm=psm: nc.scalar.copy(
                    amix.ap[:, j4 * L + tb * 128:j4 * L + tb * 128 + 128],
                    psm.ap[:, j4 * 128:j4 * 128 + 128]),
                    reads=[psm], writes=[amix])
            self.op("pe", emits[4:], reads=[zkbd, s1t[par]], writes=[psm])
            for j4 in range(4):
                self.op("act", lambda j4=j4, tb=tb, psm=psm: nc.scalar.copy(
                    amix.ap[:, (4 + j4) * L + tb * 128:
                            (4 + j4) * L + tb * 128 + 128],
                    psm.ap[:, j4 * 128:j4 * 128 + 128]),
                    reads=[psm], writes=[amix])
        wocb = [Buf(h2bf[:, i * 1024:(i + 1) * 1024], "wocb%d" % i)
                for i in (0, 1)]
        bo_bc3 = Buf(self.scr.ap[:, 3584:4096].bitcast(BF16), "bo_bc3")
        self.dma(bo_bc3, self.bo_bf_sp.broadcast_to([128, D]))
        attn_st = [Buf(self.scr.ap[:, 1536 + i * 512:2048 + i * 512]
                       .bitcast(BF16), "attn_st%d" % i) for i in (0, 1)]
        for g4 in range((NB + 3) // 4):
            gn = min(4, NB - g4 * 4)
            for kc in range(KC):
                wo_b = wocb[kc % 2]
                self.dma(wo_b, self.wo_sp[kc], q="act",
                         reads=(q_all,) if (g4 == 0 and kc < 2) else ())
                for j in range(gn):
                    tb = g4 * 4 + j
                    for half in range(2):
                        ps_w = self.pb[2 * j + half]
                        self.op("pe", lambda kc=kc, half=half, tb=tb, j=j,
                                ps_w=ps_w, wo_b=wo_b: nc.tensor.matmul(
                                    ps_w.ap,
                                    amix.ap.rearrange(
                                        "p (pr t) -> p pr t", pr=NPAIR)[
                                        :, kc, tb * 128:tb * 128 + 128],
                                    wo_b.ap[:, half * 512:half * 512 + 512],
                                    start=(kc == 0), stop=(kc == KC - 1)),
                                reads=[amix, wo_b], writes=[ps_w])
            for j in range(gn):
                tb = g4 * 4 + j
                ast = attn_st[j % 2]
                for half in range(2):
                    self.op("dve", lambda j=j, half=half, ast=ast:
                            nc.vector.tensor_tensor(
                                ast.ap[:, half * 512:half * 512 + 512],
                                self.pb[2 * j + half].ap,
                                bo_bc3.ap[:, half * 512:half * 512 + 512],
                                ALU.add),
                            reads=[self.pb[2 * j + half], bo_bc3],
                            writes=[ast])
                self.dma(self.attn_sp[s, tb], ast.ap, reads=(ast,))
        if damix is not None:
            af32 = Buf(self.scr.ap[:, 3584 - 2048 + 2048:][:, 0:0], "x")
            af32 = Buf(self.h2.ap[:, 8192:8192 + L], "amixdbg")
            for pr in range(NPAIR):
                self.op("dve", lambda pr=pr: nc.vector.tensor_copy(
                    af32.ap, amix.ap[:, pr * L:(pr + 1) * L]),
                    reads=[amix], writes=[af32])
                self.dma(damix.ap[s, pr], af32.ap, reads=(af32,))

    def phase_P4(self, s):
        nc = self.nc
        L, NB, NT, TT = self.L, self.NB, self.NT, self.TT
        h2 = self.h2
        h2q = h2.ap.bitcast(BF16)
        mix = self.mix
        scr = self.scr
        xqsw = [Buf(mix.ap[:, 0:KC * TT], "xqsw")]
        amixw = [Buf(mix.ap[:, 4096:8192], "amixw0"),
                 Buf(mix.ap[:, 12288:16384], "amixw1")]
        xo_zone = mix.ap[:, 8192:10240].bitcast(F32)
        ostage = Buf(xo_zone[:, 0:1024], "ostage")
        w2tb = [Buf(mix.ap[:, 10240 + 1024 * i:11264 + 1024 * i],
                    "w2tb%d" % i) for i in (0, 1)]
        hbuf = [Buf(scr.ap[:, i * 514:i * 514 + 514], "h%d" % i) for i in (0, 1)]
        accA = [Buf(scr.ap[:, 1028 + i * 512:1540 + i * 512], "accA%d" % i)
                for i in (0, 1)]
        accB = [Buf(scr.ap[:, 2052 + i * 512:2564 + i * 512], "accB%d" % i)
                for i in (0, 1)]
        macc = Buf(scr.ap[:, 3076:3588], "macc")
        tq = Buf(scr.ap[:, 3588:4100], "tq")
        mT = tq
        absb2 = [Buf(self.sml.ap[:, 504:1016], "absb0"), tq]
        s2bc = hbuf[0]
        seam_all = Buf(self.sml2.ap[:, 256:256 + max(1, NHC * 2 * (NT - 1))],
                       "seam_all")
        mrow = Buf(self.sml2.ap[:1, 448:448 + TT], "mrow")
        cmax = Buf(self.sml.ap[:, 90:94], "cmaxq")
        cl2 = Buf(self.sml.ap[:, 94:98], "cl2")
        r2 = Buf(self.sml.ap[:, 98:102], "r2q")
        s127 = Buf(self.sml.ap[:, 102:106], "s127")
        iv2 = Buf(self.sml.ap[:, 106:110], "iv2")
        sc2 = Buf(self.sml.ap[:, 110:114], "sc2")
        nseam = NT - 1
        NTB = TT // 128
        h2slot = [Buf(h2.ap[:, i * 512:(i + 1) * 512], "h2s%d" % i)
                  for i in range(NHC)]
        dffn = self.dbg_out("h2dbg", [self.nseq, NT, 128, NHC * TT // 4]) \
            if False else None

        if nseam > 0:
            xqseam = Buf(self.sml.ap[:, 480:480 + KC * nseam].bitcast(BF16),
                         "xqseam")
            invseam = Buf(self.sml.ap[:, 114:114 + 2 * nseam], "invseam")
            for k in range(1, NT):
                dst = xqseam.ap.rearrange(
                    "p (kc n c) -> p kc n c", kc=KC, n=nseam)[:, :, k - 1, :]
                self.dma(Buf(dst, "xqs_k"),
                         self.xqs_sp[s][:, :, k * 512 - 1:k * 512 + 1]
                         .rearrange("kc p c -> p kc c"),
                         writes=(xqseam,))
                self.op("dve", lambda k=k: nc.vector.tensor_copy(
                    invseam.ap[:, (k - 1) * 2:(k - 1) * 2 + 2],
                    self.invbc_b.ap[:, k * 512 - 1:k * 512 + 1]),
                    reads=[self.invbc_b], writes=[invseam])
            for hc in range(NHC):
                pss = self.pb[6 + hc % 2]
                emits = []
                for kc in range(KC):
                    emits.append(lambda kc=kc, hc=hc, pss=pss:
                                 nc.tensor.matmul(
                                     pss.ap[:, 0:2 * nseam],
                                     self.w1t.ap[:, kc * DH + hc * 128:
                                                 kc * DH + hc * 128 + 128],
                                     xqseam.ap[:, kc * 2 * nseam:
                                               (kc + 1) * 2 * nseam],
                                     start=(kc == 0), stop=(kc == KC - 1)))
                self.op("pe", emits, reads=[self.w1t, xqseam], writes=[pss])
                self.op("dve", lambda hc=hc, pss=pss: nc.vector.tensor_tensor(
                    seam_all.ap[:, hc * 2 * nseam:(hc + 1) * 2 * nseam],
                    pss.ap[:, 0:2 * nseam], invseam.ap, ALU.mult),
                    reads=[pss, invseam], writes=[seam_all])

        s2r_tok = [None]
        asm_q = []
        for t in range(NT):
            t0 = t * TT
            xq_b = xqsw[0]
            am_b = amixw[t % 2]
            self.dma(Buf(xq_b.ap.rearrange("p (kc c) -> p kc c", kc=KC),
                         "xqbv"),
                     self.xqs_sp[s][:, :, t0:t0 + TT].rearrange(
                         "kc p c -> p kc c"), writes=(xq_b,))
            self.dma(am_b.ap[:, 0:NTB * D].rearrange("p (tb d) -> p tb d",
                                                     tb=NTB),
                     self.attn_sp[s][t * NTB:(t + 1) * NTB].rearrange(
                         "tb p d -> p tb d"), writes=(am_b,), q="act")
            pend_max = [None]

            def stage_a(hc):
                par = hc % 2
                psm = self.pb[par]
                hcur = hbuf[par]
                emits = []
                for kc in range(KC):
                    emits.append(lambda kc=kc, hc=hc, psm=psm:
                                 nc.tensor.matmul(
                                     psm.ap[:, 0:TT],
                                     self.w1t.ap[:, kc * DH + hc * 128:
                                                 kc * DH + hc * 128 + 128],
                                     xq_b.ap[:, kc * TT:(kc + 1) * TT],
                                     start=(kc == 0), stop=(kc == KC - 1)))
                self.op("pe", emits, reads=[self.w1t, xq_b], writes=[psm])
                self.op("dve", lambda psm=psm, hcur=hcur:
                        nc.vector.tensor_tensor(
                            hcur.ap[:, 1:TT + 1], psm.ap[:, 0:TT],
                            self.invbc_b.ap[:, t0:t0 + TT], ALU.mult),
                        reads=[psm, self.invbc_b], writes=[hcur])
                if t == 0:
                    self.op("gp", lambda hcur=hcur: nc.gpsimd.memset(
                        hcur.ap[:, 0:1], 0.0), reads=[], writes=[hcur])
                if t == NT - 1:
                    self.op("gp", lambda hcur=hcur: nc.gpsimd.memset(
                        hcur.ap[:, TT + 1:TT + 2], 0.0), reads=[],
                        writes=[hcur])
                if nseam > 0:
                    if 0 < t < NT - 1:
                        sl = hc * 2 * nseam + (t - 1) * 2
                        sv = seam_all.ap[:, sl:sl + 4:3]
                        dv = hcur.ap[:, 0:TT + 2:TT + 1]
                        self.op("dve", lambda sv=sv, dv=dv:
                                nc.vector.tensor_copy(dv, sv),
                                reads=[seam_all], writes=[hcur])
                    elif 0 < t:
                        sl = hc * 2 * nseam + (t - 1) * 2
                        self.op("dve", lambda hcur=hcur, sl=sl:
                                nc.vector.tensor_copy(
                                    hcur.ap[:, 0:1],
                                    seam_all.ap[:, sl:sl + 1]),
                                reads=[seam_all], writes=[hcur])
                    elif t < NT - 1:
                        sr = hc * 2 * nseam + t * 2 + 1
                        self.op("dve", lambda hcur=hcur, sr=sr:
                                nc.vector.tensor_copy(
                                    hcur.ap[:, TT + 1:TT + 2],
                                    seam_all.ap[:, sr:sr + 1]),
                                reads=[seam_all], writes=[hcur])
                acc1 = accA[par]
                self.op("act", lambda hcur=hcur, hc=hc, acc1=acc1:
                        nc.scalar.activation(
                            acc1.ap[:, 0:TT], hcur.ap[:, 1:TT + 1],
                            AF.Identity,
                            bias=self.cb.ap[:, hc:hc + 1],
                            scale=self.cw2.ap[:, 3 * hc + 1:3 * hc + 2]),
                        reads=[hcur, self.cw2, self.cb], writes=[acc1])

            def stage_b(hc):
                par = hc % 2
                hcur = hbuf[par]
                acc1, acc2 = accA[par], accB[par]
                acc3 = acc1
                self.op("dve", lambda hcur=hcur, hc=hc, acc1=acc1, acc2=acc2:
                        nc.vector.scalar_tensor_tensor(
                            acc2.ap[:, 0:TT], hcur.ap[:, 0:TT],
                            self.cw2.ap[:, 3 * hc:3 * hc + 1],
                            acc1.ap[:, 0:TT], ALU.mult, ALU.add),
                        reads=[hcur, acc1, self.cw2], writes=[acc2])
                self.op("dve", lambda hcur=hcur, hc=hc, acc2=acc2, acc3=acc3:
                        nc.vector.scalar_tensor_tensor(
                            acc3.ap[:, 0:TT], hcur.ap[:, 2:TT + 2],
                            self.cw2.ap[:, 3 * hc + 2:3 * hc + 3],
                            acc2.ap[:, 0:TT], ALU.mult, ALU.add),
                        reads=[hcur, acc2, self.cw2], writes=[acc3])
                self.op("act", lambda hc=hc, acc3=acc3: nc.scalar.activation(
                    h2.ap[:, hc * 512:hc * 512 + TT], acc3.ap[:, 0:TT],
                    AF.Silu), reads=[acc3], writes=[h2slot[hc]])
                if hc == 0:
                    self.op("act", lambda hc=hc: nc.scalar.activation(
                        macc.ap[:, 0:TT], h2.ap[:, hc * 512:hc * 512 + TT],
                        AF.Abs), reads=[h2slot[hc]], writes=[macc])
                else:
                    absb = absb2[par]
                    aap = absb.ap[:, 0:TT]
                    self.op("act", lambda hc=hc, aap=aap: nc.scalar.activation(
                        aap, h2.ap[:, hc * 512:hc * 512 + TT],
                        AF.Abs), reads=[h2slot[hc]], writes=[absb])
                    if pend_max[0] is not None:
                        pend_max[0]()
                    pend_max[0] = (lambda aap=aap, absb=absb: self.op(
                        "dve", lambda: nc.vector.tensor_tensor(
                            macc.ap[:, 0:TT], macc.ap[:, 0:TT], aap,
                            ALU.max), reads=[absb, macc], writes=[macc]))

            stage_a(0)
            for hc in range(NHC):
                if hc + 1 < NHC:
                    stage_a(hc + 1)
                stage_b(hc)
                if asm_q and hc % 4 == 3:
                    asm_q.pop(0)()
            if pend_max[0] is not None:
                pend_max[0]()
                pend_max[0] = None
            psq = self.pb[2]
            emits = []
            for j in range(NTB):
                emits.append(lambda j=j, psq=psq: nc.tensor.transpose(
                    psq.ap[:, j * 128:(j + 1) * 128],
                    macc.ap[:, j * 128:(j + 1) * 128], self.idf_sb.ap))
            self.op("pe", emits, reads=[macc, self.idf_sb], writes=[psq])
            self.op("dve", lambda: nc.vector.tensor_copy(
                mT.ap[:, 0:TT], psq.ap[:, 0:TT]), reads=[psq], writes=[mT])
            self.op("dve", lambda: nc.vector.reduce_max(
                cmax.ap[:, 0:NTB], mT.ap[:, 0:TT].rearrange(
                    "p (j c) -> p j c", j=NTB), axis=AX.X,
                apply_absolute_value=True),
                reads=[mT], writes=[cmax])
            self.op("dve", lambda: nc.vector.tensor_scalar(
                cl2.ap[:, 0:NTB], cmax.ap[:, 0:NTB], 1e-5, None, ALU.max),
                reads=[cmax], writes=[cl2])
            self.op("dve", lambda: nc.vector.reciprocal(
                r2.ap[:, 0:NTB], cl2.ap[:, 0:NTB]), reads=[cl2], writes=[r2])
            self.op("dve", lambda: nc.vector.tensor_scalar(
                s127.ap[:, 0:NTB], r2.ap[:, 0:NTB], 127.0, None, ALU.mult),
                reads=[r2], writes=[s127])
            self.op("dve", lambda: nc.vector.tensor_scalar(
                iv2.ap[:, 0:NTB], cl2.ap[:, 0:NTB], 1.0 / 127.0, None,
                ALU.mult), reads=[cl2], writes=[iv2])
            self.op("dve", lambda: nc.vector.tensor_scalar(
                sc2.ap[:, 0:NTB], iv2.ap[:, 0:NTB], self.w2_mw.ap, None,
                ALU.mult), reads=[iv2, self.w2_mw], writes=[sc2])
            pst = self.pb[3]
            emits = []
            for j in range(NTB):
                emits.append(lambda j=j, pst=pst: nc.tensor.transpose(
                    pst.ap[:1, j * 128:j * 128 + 128], s127.ap[:, j:j + 1],
                    self.idf_sb.ap))
            self.op("pe", emits, reads=[s127, self.idf_sb], writes=[pst])
            self.op("dve", lambda: nc.vector.tensor_copy(
                mrow.ap, pst.ap[:1, 0:TT]), reads=[pst], writes=[mrow])
            tw = self.dma(self.s2r_sp, mrow.ap, reads=(mrow,),
                          wait=[s2r_tok[0]] if s2r_tok[0] else [])
            s2r_tok[0] = self.dma(Buf(s2bc.ap[:, 0:TT], "s2bcv"),
                                  self.s2r_sp.broadcast_to([128, TT]),
                                  writes=(s2bc,), wait=[tw])
            tqI = [Buf(tq.ap.bitcast(I16)[:, i * 512:(i + 1) * 512],
                       "tqI%d" % i) for i in (0, 1)]

            def q_stage(hc):
                par = hc % 2
                qa = accA[par]
                self.op("dve", lambda hc=hc, qa=qa: nc.vector.tensor_tensor(
                    qa.ap[:, 0:TT], h2.ap[:, hc * 512:hc * 512 + TT],
                    s2bc.ap[:, 0:TT], ALU.mult),
                    reads=[h2slot[hc], s2bc], writes=[qa])
                self.op("dve", lambda qa=qa, par=par: nc.vector.tensor_scalar(
                    tqI[par].ap[:, 0:TT], qa.ap[:, 0:TT], C_RND, C_RND,
                    ALU.add, ALU.subtract), reads=[qa], writes=[tqI[par]])
                self.op("act", lambda hc=hc, par=par: nc.scalar.copy(
                    h2q[:, hc * 1024:hc * 1024 + TT], tqI[par].ap[:, 0:TT]),
                    reads=[tqI[par]], writes=[h2slot[hc]])
                wb = w2tb[par]
                self.dma(wb, self.w2t_sp[hc, :, :],
                         q="sp" if par == 0 else "act")
                return wb

            def m_stage(hc, wb):
                for tb in range(NTB):
                    for half in range(2):
                        ps_o = self.pb[2 * tb + half]
                        self.op("pe", lambda hc=hc, half=half, tb=tb,
                                ps_o=ps_o, wb=wb: nc.tensor.matmul(
                                    ps_o.ap,
                                    h2q[:, hc * 1024 + tb * 128:
                                        hc * 1024 + tb * 128 + 128],
                                    wb.ap[:, half * 512:half * 512 + 512],
                                    start=(hc == 0), stop=(hc == NHC - 1)),
                                reads=[h2slot[hc], wb], writes=[ps_o])

            wb_prev = q_stage(0)
            for hc in range(NHC):
                wb_next = q_stage(hc + 1) if hc + 1 < NHC else None
                m_stage(hc, wb_prev)
                wb_prev = wb_next
            for tb in range(NTB):
                for half in range(2):
                    self.op("dve", lambda half=half, tb=tb:
                            nc.vector.tensor_scalar(
                                h2.ap[:, 12288 + tb * 1024 + half * 512:
                                      12288 + tb * 1024 + half * 512 + 512],
                                self.pb[2 * tb + half].ap,
                                sc2.ap[:, tb:tb + 1], None, ALU.mult),
                            reads=[self.pb[2 * tb + half], sc2],
                            writes=[h2slot[24 + 2 * tb + half]])
            def mk_asm(tb, t0=t0, am_b=am_b):
                def asm():
                    tok0 = t0 + tb * 128
                    xt0 = 8192 + tb * 1024
                    xsl = [h2slot[16 + 2 * tb], h2slot[17 + 2 * tb]]
                    self.dma(h2.ap[:, xt0:xt0 + 1024],
                             self.x_d[s, tok0:tok0 + 128, :], writes=xsl)
                    for half in range(2):
                        self.op("dve", lambda half=half, tb=tb, xt0=xt0:
                                nc.vector.tensor_tensor(
                                    ostage.ap[:, half * 512:half * 512 + 512],
                                    am_b.ap[:, tb * 1024 + half * 512:
                                            tb * 1024 + half * 512 + 512],
                                    h2.ap[:, xt0 + half * 512:
                                          xt0 + half * 512 + 512],
                                    ALU.add),
                                reads=[am_b, xsl[half]], writes=[ostage])
                        self.op("dve", lambda half=half, tb=tb:
                                nc.vector.tensor_tensor(
                                    ostage.ap[:, half * 512:half * 512 + 512],
                                    ostage.ap[:, half * 512:half * 512 + 512],
                                    h2.ap[:, 12288 + tb * 1024 + half * 512:
                                          12288 + tb * 1024 + half * 512 + 512],
                                    ALU.add),
                                reads=[ostage, h2slot[24 + 2 * tb + half]],
                                writes=[ostage])
                    self.dma(self.out_d[s, tok0:tok0 + 128, :], ostage.ap,
                             reads=(ostage,))
                return asm
            for tb in range(NTB):
                asm_q.append(mk_asm(tb))
        while asm_q:
            asm_q.pop(0)()


def _host_prep(inputs):
    x = np.ascontiguousarray(np.asarray(inputs["L0_attn_x"], dtype=np.float32))
    conv_w = np.asarray(inputs["L0_ffn_conv_w"], dtype=np.float32)
    conv_b = np.asarray(inputs["L0_ffn_conv_b"], dtype=np.float32)
    cw_pp = np.zeros((128, 3 * NHC), dtype=np.float32)
    cb_pp = np.zeros((128, NHC), dtype=np.float32)
    for hc in range(NHC):
        for j in range(3):
            cw_pp[:, 3 * hc + j] = conv_w[hc * 128:(hc + 1) * 128, 0, j]
        cb_pp[:, hc] = conv_b[hc * 128:(hc + 1) * 128]
    base = {
        "Wk": np.ascontiguousarray(np.asarray(inputs["L0_attn_Wk"], np.float32)),
        "Wq": np.ascontiguousarray(np.asarray(inputs["L0_attn_Wq"], np.float32)),
        "Wv": np.ascontiguousarray(np.asarray(inputs["L0_attn_Wv"], np.float32)),
        "Wo": np.ascontiguousarray(np.asarray(inputs["L0_attn_Wo"], np.float32)),
        "bo": np.ascontiguousarray(
            np.asarray(inputs["L0_attn_bo"], np.float32).reshape(1, D)),
        "w1": np.ascontiguousarray(np.asarray(inputs["L0_ffn_w1"], np.float32)),
        "w2": np.ascontiguousarray(np.asarray(inputs["L0_ffn_w2"], np.float32)),
        "convw_pp": cw_pp,
        "convb_pp": cb_pp,
    }
    return x, base


def kernel(**inputs):
    x, base = _host_prep(inputs)
    per = x.shape[0] // N_CORES
    b = Builder(L=x.shape[1], nseq=per)
    nc = b.build()
    in_maps = []
    for c in range(N_CORES):
        m = dict(base)
        m.update(b.hosts)
        m["x"] = np.ascontiguousarray(x[c * per:(c + 1) * per])
        in_maps.append(m)
    res = run_bass_kernel_spmd(nc, in_maps, core_ids=list(range(N_CORES)))
    out = np.concatenate([np.asarray(r["out"]) for r in res.results], axis=0)
    return out.astype(np.float32)


# version 37
# speedup vs baseline: 1.0349x; 1.0349x over previous
import numpy as np
import ml_dtypes
import concourse.bass as bass
import concourse.mybir as mybir
from concourse.bass_utils import run_bass_kernel_spmd

F32 = mybir.dt.float32
BF16 = mybir.dt.bfloat16
I16 = mybir.dt.int16
ALU = mybir.AluOpType
AF = mybir.ActivationFunctionType
AX = mybir.AxisListType

C_RND = 1.5 * 2.0 ** 23
N_CORES = 8
D = 1024
KC = 8
NPAIR = 8
M = 64
GRP = 32
DH = 4096
NHC = 32
PINV_ITERS = 6
EPS_LN = 1e-5
BF = ml_dtypes.bfloat16


class Tok:
    __slots__ = ("sem", "val")
    def __init__(self, sem, val):
        self.sem = sem
        self.val = val


class Buf:
    __slots__ = ("ap", "ready", "users", "name")
    def __init__(self, ap, name=""):
        self.ap = ap
        self.ready = []
        self.users = []
        self.name = name

    def view(self, ap):
        return (ap, self)


class Builder:
    def __init__(self, L=2048, nseq=2, dbg=()):
        assert L % 128 == 0
        self.L = L
        self.NB = L // 128
        self.TT = min(512, L)
        self.NT = L // self.TT
        self.nseq = nseq
        self.GRP = L // M
        self.dbg = set(dbg)
        self.nc = bass.Bass("TRN2", target_bir_lowering=False)
        nc = self.nc
        self.esem = {}
        self.ecnt = {}
        for e in ("pe", "dve", "act", "gp"):
            self.esem[e] = nc.alloc_semaphore(name="sem_" + e)
            self.ecnt[e] = 0
        self.dsem = [nc.alloc_semaphore(name="dsem%d" % i) for i in range(16)]
        self.dcnt = [0] * 16
        self.dnext = 0
        self.waited = {e: {} for e in ("pe", "dve", "act", "gp", "sp")}
        self.hosts = {}
        self.ins = {}
        self.outs = []
        self.dbg_bufs = {}

    def sb(self, name, p, f, dt):
        return Buf(self.nc.alloc_sbuf_tensor(name, [p, f], dt).ap(), name)

    def ps(self, name, p, f, dt=F32):
        return Buf(self.nc.alloc_psum_tensor(name, [p, f], dt).ap(), name)

    def dram(self, name, shape, dt, kind="Internal"):
        return self.nc.dram_tensor(name, list(shape), dt, kind=kind).ap()

    def inp(self, name, shape, dt):
        ap = self.dram(name, shape, dt, kind="ExternalInput")
        self.ins[name] = ap
        return ap

    def host_const(self, name, arr):
        dt = {np.dtype(np.float32): F32, np.dtype(BF): BF16}[arr.dtype]
        ap = self.inp(name, arr.shape, dt)
        self.hosts[name] = arr
        return ap

    def _engine(self, e):
        nc = self.nc
        return {"pe": nc.tensor, "dve": nc.vector, "act": nc.scalar,
                "gp": nc.gpsimd, "sp": nc.sync}[e]

    def _wait(self, e, tok):
        if tok is None:
            return
        key = id(tok.sem)
        w = self.waited[e]
        if w.get(key, -1) >= tok.val:
            return
        self._engine(e).wait_ge(tok.sem, tok.val)
        w[key] = tok.val

    @staticmethod
    def _buf(x):
        return x[1] if isinstance(x, tuple) else x

    def op(self, e, emits, reads=(), writes=()):
        if not isinstance(emits, (list, tuple)):
            emits = [emits]
        reads = [self._buf(b) for b in reads]
        writes = [self._buf(b) for b in writes]
        for b in reads:
            for t in b.ready:
                self._wait(e, t)
        for b in writes:
            for t in b.users:
                self._wait(e, t)
            for t in b.ready:
                self._wait(e, t)
        instr = None
        for em in emits:
            instr = em()
        sem = self.esem[e]
        self.ecnt[e] += 1
        instr.then_inc(sem, 1)
        tok = Tok(sem, self.ecnt[e])
        for b in reads:
            b.users.append(tok)
        for b in writes:
            b.ready = [tok]
            b.users = []
        return tok

    def dma(self, out, in_, reads=(), writes=(), q="sp", wait=(), **kw):
        oap = out.ap if isinstance(out, Buf) else out
        iap = in_.ap if isinstance(in_, Buf) else in_
        e = q
        reads = [self._buf(b) for b in reads]
        writes = [self._buf(b) for b in writes]
        if isinstance(in_, Buf):
            reads = reads + [in_]
        if isinstance(out, Buf):
            writes = writes + [out]
        for t in wait:
            self._wait(e, t)
        for b in reads:
            for t in b.ready:
                self._wait(e, t)
        for b in writes:
            for t in b.users:
                self._wait(e, t)
            for t in b.ready:
                self._wait(e, t)
        i = self.dnext
        self.dnext = (self.dnext + 1) % len(self.dsem)
        self._wait(e, Tok(self.dsem[i], self.dcnt[i]))
        instr = self._engine(e).dma_start(out=oap, in_=iap, **kw)
        self.dcnt[i] += 16
        instr.then_inc(self.dsem[i], 16)
        tok = Tok(self.dsem[i], self.dcnt[i])
        for b in reads:
            b.users.append(tok)
        for b in writes:
            b.ready = [tok]
            b.users = []
        return tok

    def barrier(self):
        toks = [Tok(self.esem[e], self.ecnt[e]) for e in self.esem]
        toks += [Tok(self.dsem[i], self.dcnt[i]) for i in range(len(self.dsem))]
        for e in ("pe", "dve", "act", "gp", "sp"):
            for t in toks:
                self._wait(e, t)

    def dbg_out(self, name, shape, dt=F32):
        if name not in self.dbg:
            return None
        ap = self.dram("dbg_" + name, shape, dt, kind="ExternalOutput")
        b = Buf(ap, "dbg_" + name)
        self.dbg_bufs[name] = b
        self.outs.append("dbg_" + name)
        return b

    def build(self):
        nc = self.nc
        L = self.L
        nseq = self.nseq

        self.x_d = self.inp("x", [nseq, L, D], F32)
        self.wk_d = self.inp("Wk", [D, D], F32)
        self.wq_d = self.inp("Wq", [D, D], F32)
        self.wv_d = self.inp("Wv", [D, D], F32)
        self.wo_d = self.inp("Wo", [D, D], F32)
        self.bo_d = self.inp("bo", [1, D], F32)
        self.w1_d = self.inp("w1", [DH, D], F32)
        self.w2_d = self.inp("w2", [D, DH], F32)
        self.cw_d = self.inp("convw_pp", [128, 3 * NHC], F32)
        self.cb_d = self.inp("convb_pp", [128, NHC], F32)
        self.out_d = self.dram("out", [nseq, L, D], F32, kind="ExternalOutput")
        self.outs.append("out")

        idf_h = np.eye(128, dtype=np.float32)
        idb_h = np.eye(128, dtype=BF)
        ideye_h = np.zeros((128, M), dtype=np.float32)
        for p in range(128):
            ideye_h[p, p % M] = 1.0
        masksT_h = np.zeros((2, 128), dtype=np.float32)
        masksT_h[0, :M] = 1.0
        masksT_h[1, M:] = 1.0
        self.idf_d = self.host_const("idf", idf_h)
        self.idb_d = self.host_const("idb", idb_h)
        self.ideye_d = self.host_const("ideye", ideye_h)
        self.masksT_d = self.host_const("masksT", masksT_h)

        self.xqs_sp = self.dram("xqs_sp", [nseq, KC, 128, L], BF16)
        self.amix_sp = self.dram("amix_sp", [nseq, NPAIR, 128, L], BF16)
        self.attn_sp = self.dram("attn_sp", [nseq, L // 128, 128, D], BF16)
        self.q_sp = self.dram("q_sp", [NPAIR, 128, L], BF16)
        self.w2t_sp = self.dram("w2t_sp", [NHC, 128, D], BF16)
        self.wo_sp = self.dram("wo_sp", [KC, 128, D], BF16)
        self.wkqv_sp = self.dram("wkqv_sp", [3, NPAIR, 128, KC, 128], BF16)
        self.invrow_sp = self.dram("invrow_sp", [1, L], F32)
        self.bo_bf_sp = self.dram("bo_bf_sp", [1, D], BF16)
        self.s2r_sp = self.dram("s2r_sp", [1, self.TT], F32)

        self.w1t = self.sb("w1t", 128, KC * DH, BF16)
        self.h2 = self.sb("h2", 128, 16384, F32)
        self.mix = self.sb("mix", 128, 16384, BF16)
        self.scr = self.sb("scr", 128, 4352, F32)
        self.sml = self.sb("sml", 128, 2048, F32)
        self.sml2 = self.sb("sml2", 128, 2048, F32)
        self.invbc = self.sb("invbc", 128, max(2048, L), F32)
        self.cons = self.sb("cons", 128, 1024, F32)
        self.idb_sb = self.sb("idb_sb", 128, 128, BF16)
        self.idf_sb = self.sb("idf_sb", 128, 128, F32)
        self.onesb = self.sb("onesb", 1, 128, BF16)

        self.pb = [self.ps("pb%d" % i, 128, 512, F32) for i in range(8)]

        with nc.Block() as block:
            @block.sync
            def _(sync):
                self.emit_all()
        return nc

    def emit_all(self):
        self.load_consts()
        gp, ga = self.prep_weights(), self.phase_A(0)
        live = {"p": gp, "a": ga}
        while live:
            for key, k in (("p", 2), ("a", 1)):
                g = live.get(key)
                if g is None:
                    continue
                for _ in range(k):
                    try:
                        next(g)
                    except StopIteration:
                        del live[key]
                        break
        self.barrier()
        for s in range(self.nseq):
            if s > 0:
                self.phase_A_run(s)
                self.barrier()
            self.phase_B1(s)
            self.barrier()
            self.phase_B2(s)
            self.barrier()
            self.phase_B3(s)
            self.barrier()
            self.phase_P4(s)
            self.barrier()

    def load_consts(self):
        nc = self.nc
        self.dma(self.idb_sb, self.idb_d)
        self.dma(self.idf_sb, self.idf_d)
        self.op("dve", lambda: nc.vector.memset(self.onesb.ap, 1.0),
                writes=[self.onesb])
        self.ones_col = Buf(self.cons.ap[:, 0:1], "ones")
        self.op("dve", lambda: nc.vector.memset(self.ones_col.ap, 1.0),
                writes=[self.ones_col])
        self.ideye = Buf(self.cons.ap[:, 8:8 + M], "ideye")
        self.dma(self.ideye, self.ideye_d)
        self.i7 = Buf(self.cons.ap[:, 72:72 + M], "i7")
        self.i15 = Buf(self.cons.ap[:, 136:136 + M], "i15")
        self.i13 = Buf(self.cons.ap[:, 200:200 + M], "i13")
        for b, c in ((self.i7, 7.0), (self.i15, 15.0), (self.i13, 13.0)):
            self.op("dve", lambda b=b, c=c: nc.vector.tensor_scalar(
                b.ap, self.ideye.ap, c, None, ALU.mult),
                reads=[self.ideye], writes=[b])
        self.crnd_col = Buf(self.cons.ap[:, 394:395], "crnd")
        self.op("dve", lambda: nc.vector.memset(self.crnd_col.ap, C_RND),
                writes=[self.crnd_col])
        self.epsln_col = Buf(self.cons.ap[:, 395:396], "epsln")
        self.op("dve", lambda: nc.vector.memset(self.epsln_col.ap, EPS_LN),
                writes=[self.epsln_col])
        self.neghalf_col = Buf(self.cons.ap[:, 396:397], "neghalf")
        self.op("dve", lambda: nc.vector.memset(self.neghalf_col.ap, -0.5),
                writes=[self.neghalf_col])
        self.q8_col = Buf(self.cons.ap[:, 397:398], "q8c")
        self.op("dve", lambda: nc.vector.memset(self.q8_col.ap, 0.125),
                writes=[self.q8_col])
        self.masksT = Buf(self.cons.ap[:2, 264:392], "masksT")
        self.dma(self.masksT, self.masksT_d)
        self.masks = Buf(self.cons.ap[:, 392:394], "masks")
        self.op("dve", lambda: nc.vector.memset(self.masks.ap, 0.0),
                writes=[self.masks])
        self.op("dve", lambda: nc.vector.memset(self.cons.ap[:M, 392:393], 1.0),
                reads=[self.masks], writes=[self.masks])
        self.op("dve", lambda: nc.vector.memset(self.cons.ap[M:, 393:394], 1.0),
                reads=[self.masks], writes=[self.masks])

    def prep_weights(self):
        nc = self.nc
        h2 = self.h2
        wtmp = Buf(h2.ap[:, 0:4096], "wtmp")
        wbf = Buf(h2.ap[:, 14336:14848].bitcast(BF16), "wbf")
        wtmpq = [Buf(h2.ap[:, i * 1024:(i + 1) * 1024], "wtq%d" % i)
                 for i in (0, 1)]
        wbfq = [Buf(h2.ap[:, 14336 + i * 512:14848 + i * 512].bitcast(BF16),
                    "wbfq%d" % i) for i in (0, 1)]
        WDS = ((self.wk_d, None), (self.wq_d, self.q8_col),
               (self.wv_d, None), (self.wo_d, None))

        def qkv_ld(j):
            wi, kc = divmod(j, KC)
            self.dma(wtmpq[j % 2], WDS[wi][0][kc * 128:(kc + 1) * 128, :])

        qkv_ld(0)
        for j in range(4 * KC):
            wi, kc = divmod(j, KC)
            if j + 1 < 4 * KC:
                qkv_ld(j + 1)
            wt, wb2 = wtmpq[j % 2], wbfq[j % 2]
            sc = WDS[wi][1]
            if sc is None:
                self.op("act", lambda wt=wt, wb2=wb2: nc.scalar.copy(
                    wb2.ap, wt.ap), reads=[wt], writes=[wb2])
            else:
                self.op("act", lambda wt=wt, wb2=wb2, sc=sc:
                        nc.scalar.activation(wb2.ap, wt.ap, AF.Identity,
                                             scale=sc.ap),
                        reads=[wt, sc], writes=[wb2])
            if wi == 3:
                self.dma(self.wo_sp[kc], wb2.ap, reads=(wb2,), q="act")
            else:
                self.dma(self.wkqv_sp[wi, :, :, kc, :].rearrange(
                    "pr p c -> p pr c"),
                    wb2.ap.rearrange("p (pr c) -> p pr c", pr=NPAIR),
                    reads=(wb2,), q="act")
            yield
        bo_row = Buf(self.sml2.ap[:1, 960:1984], "bo_row")
        self.dma(bo_row, self.bo_d)
        bo_rbf = Buf(self.sml2.ap[:1, 448:960].bitcast(BF16), "bo_rbf")
        self.op("act", lambda: nc.scalar.copy(bo_rbf.ap, bo_row.ap),
                reads=[bo_row], writes=[bo_rbf])
        self.dma(self.bo_bf_sp, bo_rbf.ap, reads=(bo_rbf,))
        self.cw = Buf(self.sml2.ap[:, 0:3 * NHC], "cw")
        self.cb = Buf(self.sml2.ap[:, 96:96 + NHC], "cb")
        self.dma(self.cw, self.cw_d)
        self.dma(self.cb, self.cb_d)
        tern_bufs = dict(
            wtmp=[Buf(h2.ap[:, i * 4096:(i + 1) * 4096], "twtmp%d" % i)
                  for i in (0, 1)],
            ti=Buf(h2.ap[:, 8192:10240].bitcast(I16), "ti"),
            tb=Buf(h2.ap[:, 10240:12288].bitcast(BF16), "tb"),
            stage=Buf(h2.ap[:, 12288:14336].bitcast(BF16), "tstage"),
        )
        yield from self._tern_weight(self.w1_d, DH, D, True, tern_bufs)
        yield from self._tern_weight(self.w2_d, D, DH, False, tern_bufs)

    def _tern_weight(self, wd, nrows, ncol, is_w1, tern_bufs):
        nc = self.nc
        nch = nrows // 128
        wtmps, ti, tb, stage = (tern_bufs[k] for k in
                                ("wtmp", "ti", "tb", "stage"))
        absacc = Buf(self.sml.ap[:, 448:448 + nch], "absacc")
        if is_w1:
            self._absacc = absacc
        else:
            absacc = self._absacc
        def ldw(c):
            self.dma(Buf(wtmps[c % 2].ap[:, 0:ncol], "wtv"),
                     wd[c * 128:(c + 1) * 128, :], writes=(wtmps[c % 2],))

        ldw(0)
        for c in range(nch):
            if c + 1 < nch:
                ldw(c + 1)
            wtmp = wtmps[c % 2]
            self.op("dve", lambda c=c, wtmp=wtmp: nc.vector.reduce_sum(
                absacc.ap[:, c:c + 1], wtmp.ap[:, 0:ncol], axis=AX.X,
                apply_absolute_value=True),
                reads=[wtmp], writes=[absacc])
            yield
        ps = self.pb[0]
        self.op("pe", lambda: nc.tensor.matmul(
            ps.ap[:1, 0:nch], self.ones_col.ap, absacc.ap[:, 0:nch],
            start=True, stop=True),
            reads=[self.ones_col, absacc], writes=[ps])
        tot = Buf(self.sml.ap[:1, 1700:1701], "tot")
        self.op("dve", lambda: nc.vector.reduce_sum(
            tot.ap, ps.ap[:1, 0:nch], axis=AX.X), reads=[ps], writes=[tot])
        mean = Buf(self.sml.ap[:1, 1701:1702], "mean")
        self.op("dve", lambda: nc.vector.tensor_scalar(
            mean.ap, tot.ap, 1.0 / (nrows * ncol), 1e-5, ALU.mult, ALU.max),
            reads=[tot], writes=[mean])
        scale = Buf(self.sml.ap[:1, 1702:1703], "scale")
        self.op("dve", lambda: nc.vector.reciprocal(scale.ap, mean.ap),
                reads=[mean], writes=[scale])
        row2 = Buf(self.sml.ap[:1, 1704:1706], "row2")
        self.op("dve", lambda: nc.vector.tensor_copy(row2.ap[:, 0:1], scale.ap),
                reads=[scale], writes=[row2])
        self.op("dve", lambda: nc.vector.tensor_copy(row2.ap[:, 1:2], mean.ap),
                reads=[mean, row2], writes=[row2])
        ones_row = Buf(self.sml.ap[:1, 1708:1836], "ones_row")
        self.op("dve", lambda: nc.vector.memset(ones_row.ap, 1.0),
                writes=[ones_row])
        ps2 = self.pb[1]
        self.op("pe", lambda: nc.tensor.matmul(
            ps2.ap[:, 0:2], ones_row.ap, row2.ap, start=True, stop=True),
            reads=[ones_row, row2], writes=[ps2])
        if is_w1:
            self.w1_mw = Buf(self.sml2.ap[:, 229:230], "w1_mw")
            self.w1_sc = Buf(self.sml2.ap[:, 230:231], "w1_sc")
            mw_col, sc_col = self.w1_mw, self.w1_sc
        else:
            self.w2_mw = Buf(self.sml2.ap[:, 231:232], "w2_mw")
            self.w2_sc = Buf(self.sml2.ap[:, 232:233], "w2_sc")
            mw_col, sc_col = self.w2_mw, self.w2_sc
        self.op("dve", lambda: nc.vector.tensor_copy(sc_col.ap, ps2.ap[:, 0:1]),
                reads=[ps2], writes=[sc_col])
        self.op("dve", lambda: nc.vector.tensor_copy(mw_col.ap, ps2.ap[:, 1:2]),
                reads=[ps2], writes=[mw_col])
        if is_w1:
            self.cw2 = Buf(self.sml2.ap[:, 128:128 + 3 * NHC], "cw2")
            self.op("dve", lambda: nc.vector.tensor_scalar(
                self.cw2.ap, self.cw.ap, mw_col.ap, None, ALU.mult),
                reads=[self.cw, mw_col], writes=[self.cw2])
        nblk = ncol // 128
        ldw(0)
        for c in range(nch):
            if c + 1 < nch:
                ldw(c + 1)
            wtmp = wtmps[c % 2]
            self.op("act", lambda wtmp=wtmp: nc.scalar.activation(
                wtmp.ap[:, 0:ncol], wtmp.ap[:, 0:ncol], AF.Identity,
                scale=sc_col.ap, bias=self.crnd_col.ap),
                reads=[wtmp, sc_col, self.crnd_col], writes=[wtmp])
            self.op("dve", lambda wtmp=wtmp: nc.vector.tensor_scalar(
                ti.ap[:, 0:ncol], wtmp.ap[:, 0:ncol], C_RND, None,
                ALU.subtract), reads=[wtmp], writes=[ti])
            self.op("dve", lambda: nc.vector.tensor_scalar(
                tb.ap[:, 0:ncol], ti.ap[:, 0:ncol], -1.0, 1.0, ALU.max,
                ALU.min), reads=[ti], writes=[tb])
            for g0 in range(0, nblk, 8):
                gn = min(8, nblk - g0)
                psb = self.pb[2 + (g0 // 8) % 2]
                pbf = psb.ap.bitcast(BF16)
                emits = []
                for j in range(gn):
                    blk = g0 + j
                    emits.append(lambda j=j, blk=blk: nc.tensor.transpose(
                        pbf[:, j * 128:(j + 1) * 128],
                        tb.ap[:, blk * 128:(blk + 1) * 128],
                        self.idb_sb.ap))
                self.op("pe", emits, reads=[tb, self.idb_sb], writes=[psb])
                if is_w1:
                    dst = self.w1t.ap.rearrange("p (kc d) -> p kc d", kc=KC)[
                        :, g0:g0 + gn, c * 128:(c + 1) * 128]
                    self.op("dve", lambda dst=dst, gn=gn, pbf=pbf:
                            nc.vector.tensor_copy(
                                dst, pbf[:, 0:gn * 128].rearrange(
                                    "p (kc d) -> p kc d", kc=gn)),
                            reads=[psb], writes=[self.w1t])
                else:
                    self.op("dve", lambda g0=g0, gn=gn, pbf=pbf:
                            nc.vector.tensor_copy(
                                stage.ap[:, g0 * 128:(g0 + gn) * 128],
                                pbf[:, 0:gn * 128]),
                            reads=[psb], writes=[stage])
            if not is_w1:
                self.dma(self.w2t_sp[:, :, c * 128:(c + 1) * 128].rearrange(
                    "hc p c2 -> p hc c2"),
                    stage.ap[:, 0:NHC * 128].rearrange(
                        "p (hc c2) -> p hc c2", hc=NHC),
                    reads=(stage,), q="act")
            yield

    def phase_A_run(self, s):
        for _ in self.phase_A(s):
            pass

    def phase_A(self, s):
        nc = self.nc
        L, NB = self.L, self.NB
        scr, sml = self.scr, self.sml
        self.invsx_all = Buf(sml.ap[:, 0:NB], "invsx_all")
        xt = [Buf(scr.ap[:, 0:1024], "xt0"),
              Buf(self.invbc.ap[:, 0:1024], "xt1")]
        xn = [Buf(scr.ap[:, 1024:2048], "xn0"),
              Buf(self.invbc.ap[:, 1024:2048], "xn1")]
        zone2 = [Buf(scr.ap[:, 2048:2560], "zone2_0"),
                 Buf(sml.ap[:, 512:1024], "zone2_1")]
        xqb = [Buf(scr.ap[:, 2560:3072].bitcast(BF16), "xqb0"),
               Buf(sml.ap[:, 1024:1536].bitcast(BF16), "xqb1")]
        xsq_sh = Buf(scr.ap[:, 3072:4096], "xsq_sh")
        rstage = [Buf(sml.ap[:1, 128:256], "rstage0"),
                  Buf(sml.ap[:1, 256:384], "rstage1")]
        stats = [Buf(sml.ap[:, 64:80], "stats0"), Buf(sml.ap[:, 32:48], "st1")]
        CELLS = ("mean", "rsq", "var", "lnv", "rstd", "mb", "m2",
                 "amax", "clip", "rcl", "sx")
        cells = [{n: Buf(stats[p].ap[:, i:i + 1], n + str(p))
                  for i, n in enumerate(CELLS)} for p in (0, 1)]
        xn_fm = Buf(self.mix.ap[:, 0:KC * L], "xn_fm")
        self.xn_fm = xn_fm
        dxn = self.dbg_out("xn_tm", [self.nseq, L, D]) if s == 0 else \
            self.dbg_bufs.get("xn_tm")
        invrow_toks = []

        def load(b):
            self.dma(xt[b % 2], self.x_d[s, b * 128:b * 128 + 128, :])

        def hA(b):
            par = b % 2
            c = cells[par]
            xtb, xnb_ = xt[par], xn[par]
            self.op("dve", lambda: nc.vector.reduce_sum(
                c["mean"].ap, xtb.ap, axis=AX.X), reads=[xtb],
                writes=[c["mean"]])
            self.op("act", lambda: nc.scalar.activation(
                xsq_sh.ap, xtb.ap, AF.Square, accum_out=c["rsq"].ap),
                reads=[xtb], writes=[xsq_sh, c["rsq"]])
            self.op("dve", lambda: nc.vector.tensor_scalar(
                c["m2"].ap, c["mean"].ap, c["mean"].ap, 1.0 / (D * D),
                ALU.mult, ALU.mult), reads=[c["mean"]], writes=[c["m2"]])
            self.op("dve", lambda: nc.vector.tensor_scalar(
                c["var"].ap, c["rsq"].ap, 1.0 / D, c["m2"].ap,
                ALU.mult, ALU.subtract), reads=[c["rsq"], c["m2"]],
                writes=[c["var"]])
            self.op("act", lambda: nc.scalar.activation(
                c["lnv"].ap, c["var"].ap, AF.Ln, bias=self.epsln_col.ap),
                reads=[c["var"], self.epsln_col], writes=[c["lnv"]])
            self.op("act", lambda: nc.scalar.activation(
                c["rstd"].ap, c["lnv"].ap, AF.Exp, scale=self.neghalf_col.ap),
                reads=[c["lnv"], self.neghalf_col], writes=[c["rstd"]])
            self.op("dve", lambda: nc.vector.tensor_scalar(
                c["mb"].ap, c["mean"].ap, c["rstd"].ap, -1.0 / D,
                ALU.mult, ALU.mult), reads=[c["mean"], c["rstd"]],
                writes=[c["mb"]])
            self.op("act", lambda: nc.scalar.activation(
                xnb_.ap, xtb.ap, AF.Identity, scale=c["rstd"].ap,
                bias=c["mb"].ap), reads=[xtb, c["rstd"], c["mb"]],
                writes=[xnb_])
            if dxn is not None:
                self.dma(dxn.ap[s, b * 128:b * 128 + 128, :], xn[par].ap,
                         reads=(xn[par],))

        def hB(b):
            par = b % 2
            c = cells[par]
            xtb, xnb_ = xt[par], xn[par]
            z2 = zone2[par]
            ui_ap = z2.ap.bitcast(I16)
            self.op("dve", lambda: nc.vector.reduce_max(
                c["amax"].ap, xnb_.ap, axis=AX.X, apply_absolute_value=True),
                reads=[xnb_], writes=[c["amax"]])
            self.op("dve", lambda: nc.vector.tensor_scalar(
                c["clip"].ap, c["amax"].ap, 1e-5, None, ALU.max),
                reads=[c["amax"]], writes=[c["clip"]])
            self.op("dve", lambda: nc.vector.reciprocal(
                c["rcl"].ap, c["clip"].ap), reads=[c["clip"]],
                writes=[c["rcl"]])
            self.op("dve", lambda: nc.vector.tensor_scalar(
                c["sx"].ap, c["rcl"].ap, 127.0, None, ALU.mult),
                reads=[c["rcl"]], writes=[c["sx"]])
            self.op("dve", lambda b=b: nc.vector.tensor_scalar(
                self.invsx_all.ap[:, b:b + 1], c["clip"].ap, 1.0 / 127.0,
                None, ALU.mult), reads=[c["clip"]], writes=[self.invsx_all])
            self.op("dve", lambda: nc.vector.tensor_scalar(
                xsq_sh.ap, xnb_.ap, c["sx"].ap, C_RND, ALU.mult, ALU.add),
                reads=[xnb_, c["sx"]], writes=[xsq_sh])
            self.op("dve", lambda ui_ap=ui_ap: nc.vector.tensor_scalar(
                ui_ap, xsq_sh.ap, C_RND, None, ALU.subtract),
                reads=[xsq_sh], writes=[z2])
            self.op("act", lambda ui_ap=ui_ap, par=par: nc.scalar.copy(
                xqb[par].ap, ui_ap), reads=[z2], writes=[xqb[par]])
            self.op("act", lambda par=par: nc.scalar.copy(
                zone2[par].ap.bitcast(BF16), xn[par].ap),
                reads=[xn[par]], writes=[z2])

        def hC(b):
            par = b % 2
            t0 = b * 128
            for idx, src in ((0, zone2[par]), (1, xqb[par])):
                psb = self.pb[4 + 2 * par + idx]
                pbf = psb.ap.bitcast(BF16)
                emits = []
                sap = src.ap.bitcast(BF16) if idx == 0 else src.ap
                for kc in range(KC):
                    emits.append(lambda kc=kc, sap=sap, pbf=pbf:
                                 nc.tensor.transpose(
                                     pbf[:, kc * 128:(kc + 1) * 128],
                                     sap[:, kc * 128:(kc + 1) * 128],
                                     self.idb_sb.ap))
                self.op("pe", emits, reads=[src, self.idb_sb], writes=[psb])
                if idx == 0:
                    dst = xn_fm.ap.rearrange("p (kc t) -> p kc t", kc=KC)[
                        :, :, t0:t0 + 128]
                    self.op("act", lambda dst=dst, pbf=pbf:
                            nc.scalar.copy(
                                dst, pbf.rearrange("p (kc c) -> p kc c",
                                                   kc=KC)),
                            reads=[psb], writes=[xn_fm])
                else:
                    self.op("act", lambda pbf=pbf, par=par: nc.scalar.copy(
                        xqb[par].ap, pbf), reads=[psb], writes=[xqb[par]])
                    self.dma(self.xqs_sp[s, :, :, t0:t0 + 128].rearrange(
                        "kc p c -> p kc c"),
                        xqb[par].ap.rearrange("p (kc c) -> p kc c", kc=KC),
                        reads=(xqb[par],), q="act")
            psr = self.pb[5 + 2 * par]
            self.op("pe", lambda b=b, psr=psr: nc.tensor.transpose(
                psr.ap[:1, 0:128], self.invsx_all.ap[:, b:b + 1],
                self.idf_sb.ap), reads=[self.invsx_all, self.idf_sb],
                writes=[psr])
            self.op("act", lambda psr=psr, par=par: nc.scalar.copy(
                rstage[par].ap, psr.ap[:1, 0:128]), reads=[psr],
                writes=[rstage[par]])
            invrow_toks.append(
                self.dma(self.invrow_sp[:, t0:t0 + 128], rstage[par].ap,
                         reads=(rstage[par],)))

        load(0)
        for b in range(NB):
            if b + 1 < NB:
                load(b + 1)
            hA(b)
            if b >= 1:
                hB(b - 1)
            if b >= 2:
                hC(b - 2)
            yield
        hB(NB - 1)
        hC(NB - 2)
        hC(NB - 1)
        ivb = Buf(self.invbc.ap[:, 0:L], "invbc")
        self.dma(ivb, self.invrow_sp.broadcast_to([128, L]),
                 wait=invrow_toks, writes=(xt[1], xn[1]))
        self.invbc_b = ivb

    def phase_B1(self, s):
        nc = self.nc
        L, NB = self.L, self.NB
        xn_fm = self.xn_fm
        h2 = self.h2
        h2bf = h2.ap.bitcast(BF16)
        kp = [Buf(h2bf[:, i * 2048:i * 2048 + L], "k_p%d" % i) for i in (0, 1)]
        vp = [Buf(h2bf[:, 4096 + i * 2048:4096 + i * 2048 + L], "v_p%d" % i)
              for i in (0, 1)]
        p3 = Buf(h2bf[:, 8192:8192 + L], "p3")
        p3t = Buf(h2bf[:, 10240:10240 + L], "p3t")
        qstage = [Buf(h2bf[:, 12288 + i * 512:12288 + (i + 1) * 512],
                      "qst%d" % i) for i in (0, 1)]
        wpb = [[Buf(h2bf[:, base + i * 1024:base + (i + 1) * 1024],
                    "wp%d_%d" % (g, i)) for i in range(3)]
               for g, base in ((0, 13312), (1, 18432))]
        h2f = self.h2.ap
        self.s2bd_all = Buf(h2f[:, 8192:9216], "s2bd_all")
        lam3 = Buf(self.scr.ap[:, 0:L], "lam3")
        xl = Buf(self.scr.ap[:, 2048:2560], "xl")
        xlb = Buf(self.scr.ap[:, 2560:2816].bitcast(BF16), "xlb")
        s2e = Buf(self.scr.ap[:, 2816:2880], "s2e")
        self.O_all = Buf(self.sml2.ap[:, 256:768], "O_all")
        self.s2n_all = Buf(self.sml2.ap[:, 768:1280], "s2n")
        self.s2T_all = Buf(self.sml2.ap[:, 1280:1792], "s2T")
        qlkl = self.sml2.ap[:, 1792:2048].bitcast(BF16)
        self.ql_all = Buf(qlkl[:, 0:NPAIR * M], "ql_all")
        kl_region = self.sml.ap[:, 1792:2048].bitcast(BF16)
        self.kl_all = Buf(kl_region[:, 0:NPAIR * M], "kl_all")
        rs3 = [Buf(self.sml.ap[:, 80 + i:81 + i], "rs3_%d" % i) for i in (0, 1)]
        rmax3 = Buf(self.sml.ap[:, 82:83], "rmax3")
        nmax3 = Buf(self.sml.ap[:, 83:84], "nmax3")
        rc3 = [Buf(self.sml.ap[:, 84 + i:85 + i], "rc3_%d" % i) for i in (0, 1)]
        s2m = Buf(self.sml.ap[:, 96:97], "s2m")
        s2nm = Buf(self.sml.ap[:, 97:98], "s2nm")
        s2s = Buf(self.sml.ap[:, 98:99], "s2s")
        s2r = Buf(self.sml.ap[:, 99:100], "s2r")
        bd = Buf(self.sml.ap[:, 256:320].bitcast(BF16), "bd")
        self.op("dve", lambda: nc.vector.memset(bd.ap, 0.0), writes=[bd])

        for kc in range(KC):
            self.op("dve", lambda kc=kc: nc.vector.reduce_sum(
                xl.ap[:, kc * M:(kc + 1) * M],
                xn_fm.ap[:, kc * L:(kc + 1) * L].rearrange(
                    "p (g w) -> p g w", w=self.GRP), axis=AX.X),
                reads=[xn_fm], writes=[xl])
        self.op("dve", lambda: nc.vector.tensor_scalar(
            xlb.ap, xl.ap, 1.0 / self.GRP, None, ALU.mult),
            reads=[xl], writes=[xlb])

        ds2 = self.dbg_out("s2n", [self.nseq, 128, 512]) if s == 0 else \
            self.dbg_bufs.get("s2n")
        dO = self.dbg_out("O_all", [self.nseq, 128, 512]) if s == 0 else \
            self.dbg_bufs.get("O_all")

        def tail(pr):
            par = pr % 2
            for g in range(2):
                psb = self.pb[4 + g]
                pbf = psb.ap.bitcast(BF16)
                emits = []
                for j in range(8):
                    tb = g * 8 + j
                    if tb >= NB:
                        break
                    emits.append(lambda j=j, tb=tb, pbf=pbf:
                                 nc.tensor.transpose(
                                     pbf[:, j * 128:(j + 1) * 128],
                                     p3.ap[:, tb * 128:tb * 128 + 128],
                                     self.idb_sb.ap))
                if not emits:
                    break
                gn = len(emits)
                self.op("pe", emits, reads=[p3, self.idb_sb], writes=[psb])
                self.op("dve", lambda g=g, gn=gn, pbf=pbf:
                        nc.vector.tensor_copy(
                            p3t.ap[:, g * 1024:g * 1024 + gn * 128],
                            pbf[:, 0:gn * 128]),
                        reads=[psb], writes=[p3t])
            psO = self.pb[6]
            emits = []
            for tb in range(NB):
                emits.append(lambda tb=tb, psO=psO, par=par: nc.tensor.matmul(
                    psO.ap[:, 0:128], p3t.ap[:, tb * 128:tb * 128 + 128],
                    vp[par].ap[:, tb * 128:tb * 128 + 128],
                    start=(tb == 0), stop=(tb == NB - 1)))
            self.op("pe", emits, reads=[p3t, vp[par]], writes=[psO])
            self.op("dve", lambda pr=pr, par=par: nc.vector.tensor_scalar(
                self.O_all.ap[0:64, pr * M:(pr + 1) * M], psO.ap[0:64, 0:64],
                rc3[par].ap[0:64, :], None, ALU.mult),
                reads=[psO, rc3[par]], writes=[self.O_all])
            self.op("dve", lambda pr=pr, par=par: nc.vector.tensor_scalar(
                self.O_all.ap[64:128, pr * M:(pr + 1) * M],
                psO.ap[64:128, 64:128], rc3[par].ap[64:128, :], None,
                ALU.mult), reads=[psO, rc3[par]], writes=[self.O_all])

        def wload(pr):
            ws = wpb[pr % 2]
            for wi, wb in ((1, ws[0]), (0, ws[1]), (2, ws[2])):
                self.dma(Buf(wb.ap, wb.name),
                         self.wkqv_sp[wi, pr].rearrange("p kc c -> p (kc c)"),
                         writes=(wb,), q="act" if wi != 0 else "sp")

        wload(0)
        for pr in range(NPAIR):
            par = pr % 2
            wq_b, wk_b, wv_b = wpb[par]
            if pr + 1 < NPAIR:
                wload(pr + 1)
            for (wb, dst) in ((wq_b, None), (wk_b, kp[par])):
                for n4 in range(max(1, L // 512)):
                    w512 = min(512, L)
                    psq = self.pb[n4 % 2]
                    emits = []
                    for kc in range(KC):
                        emits.append(lambda kc=kc, n4=n4, psq=psq, wb=wb,
                                     w512=w512: nc.tensor.matmul(
                                         psq.ap[:, 0:w512],
                                         wb.ap[:, kc * 128:(kc + 1) * 128],
                                         xn_fm.ap[:, kc * L + n4 * 512:
                                                  kc * L + n4 * 512 + w512],
                                         start=(kc == 0), stop=(kc == KC - 1)))
                    self.op("pe", emits, reads=[wb, xn_fm], writes=[psq])
                    if dst is None:
                        qs = qstage[n4 % 2]
                        self.op("act", lambda n4=n4, psq=psq, qs=qs, w512=w512:
                                nc.scalar.copy(
                                    qs.ap[:, 0:w512], psq.ap[:, 0:w512]),
                                reads=[psq], writes=[qs])
                        self.dma(self.q_sp[pr][:, n4 * 512:n4 * 512 + w512],
                                 qs.ap[:, 0:w512], reads=(qs,), q="act")
                    else:
                        self.op("act", lambda n4=n4, psq=psq, dst=dst,
                                w512=w512: nc.scalar.copy(
                                    dst.ap[:, n4 * 512:n4 * 512 + w512],
                                    psq.ap[:, 0:w512]),
                                reads=[psq], writes=[dst])
            for g4 in range((NB + 3) // 4):
                psv = self.pb[2 + g4 % 2]
                emits = []
                gn = min(4, NB - g4 * 4)
                for j in range(gn):
                    tb = g4 * 4 + j
                    for kc in range(KC):
                        emits.append(lambda kc=kc, tb=tb, j=j, psv=psv,
                                     wv_b=wv_b: nc.tensor.matmul(
                                         psv.ap[:, j * 128:j * 128 + 128],
                                         xn_fm.ap[:, kc * L + tb * 128:
                                                  kc * L + tb * 128 + 128],
                                         wv_b.ap[:, kc * 128:(kc + 1) * 128],
                                         start=(kc == 0), stop=(kc == KC - 1)))
                self.op("pe", emits, reads=[wv_b, xn_fm], writes=[psv])
                self.op("act", lambda g4=g4, gn=gn, psv=psv, par=par:
                        nc.scalar.copy(
                            vp[par].ap[:, g4 * 512:g4 * 512 + gn * 128],
                            psv.ap[:, 0:gn * 128]),
                        reads=[psv], writes=[vp[par]])
            for (wb, dstall) in ((wq_b, self.ql_all), (wk_b, self.kl_all)):
                psl = self.pb[4]
                emits = []
                for kc in range(KC):
                    emits.append(lambda kc=kc, psl=psl, wb=wb:
                                 nc.tensor.matmul(
                                     psl.ap[:, 0:M],
                                     wb.ap[:, kc * 128:(kc + 1) * 128],
                                     xlb.ap[:, kc * M:(kc + 1) * M],
                                     start=(kc == 0), stop=(kc == KC - 1)))
                self.op("pe", emits, reads=[wb, xlb], writes=[psl])
                self.op("dve", lambda psl=psl, dstall=dstall, pr=pr:
                        nc.vector.tensor_copy(
                            dstall.ap[:, pr * M:(pr + 1) * M], psl.ap[:, 0:M]),
                        reads=[psl], writes=[dstall])
            self.op("dve", lambda pr=pr: nc.vector.tensor_copy(
                bd.ap[0:64, 0:64], self.ql_all.ap[0:64, pr * M:(pr + 1) * M]),
                reads=[self.ql_all, bd], writes=[bd])
            self.op("dve", lambda pr=pr: nc.vector.tensor_copy(
                bd.ap[64:128, 64:128],
                self.ql_all.ap[64:128, pr * M:(pr + 1) * M]),
                reads=[self.ql_all, bd], writes=[bd])
            ps2l = self.pb[5]
            self.op("pe", lambda pr=pr: nc.tensor.matmul(
                ps2l.ap[:, 0:M], bd.ap, self.kl_all.ap[:, pr * M:(pr + 1) * M],
                start=True, stop=True),
                reads=[bd, self.kl_all], writes=[ps2l])
            self.op("dve", lambda: nc.vector.reduce_max(
                s2m.ap, ps2l.ap[:, 0:M], axis=AX.X), reads=[ps2l], writes=[s2m])
            self.op("dve", lambda: nc.vector.tensor_scalar(
                s2nm.ap, s2m.ap, -1.0, None, ALU.mult), reads=[s2m],
                writes=[s2nm])
            self.op("act", lambda: nc.scalar.activation(
                s2e.ap, ps2l.ap[:, 0:M], AF.Exp, bias=s2nm.ap,
                accum_out=s2s.ap), reads=[ps2l, s2nm], writes=[s2e, s2s])
            self.op("dve", lambda: nc.vector.reciprocal(s2r.ap, s2s.ap),
                    reads=[s2s], writes=[s2r])
            self.op("dve", lambda pr=pr: nc.vector.tensor_scalar(
                self.s2n_all.ap[:, pr * M:(pr + 1) * M], s2e.ap, s2r.ap, None,
                ALU.mult), reads=[s2e, s2r], writes=[self.s2n_all])
            for n4 in range(max(1, L // 512)):
                w512 = min(512, L)
                psq = self.pb[n4 % 2]
                self.op("pe", lambda psq=psq, n4=n4, w512=w512, par=par:
                        nc.tensor.matmul(
                            psq.ap[:, 0:w512], bd.ap,
                            kp[par].ap[:, n4 * 512:n4 * 512 + w512],
                            start=True, stop=True),
                        reads=[bd, kp[par]], writes=[psq])
                self.op("act", lambda psq=psq, n4=n4, w512=w512:
                        nc.scalar.copy(
                            lam3.ap[:, n4 * 512:n4 * 512 + w512],
                            psq.ap[:, 0:w512]),
                        reads=[psq], writes=[lam3])
            if pr > 0:
                tail(pr - 1)
            self.op("dve", lambda: nc.vector.reduce_max(
                rmax3.ap, lam3.ap, axis=AX.X), reads=[lam3], writes=[rmax3])
            self.op("dve", lambda: nc.vector.tensor_scalar(
                nmax3.ap, rmax3.ap, -1.0, None, ALU.mult),
                reads=[rmax3], writes=[nmax3])
            self.op("act", lambda par=par: nc.scalar.activation(
                p3.ap, lam3.ap, AF.Exp, bias=nmax3.ap,
                accum_out=rs3[par].ap),
                reads=[lam3, nmax3], writes=[p3, rs3[par]])
            self.op("dve", lambda par=par: nc.vector.reciprocal(
                rc3[par].ap, rs3[par].ap), reads=[rs3[par]], writes=[rc3[par]])
        tail(NPAIR - 1)
        bdp = Buf(h2f[:, 9216:10240], "bd_pre")
        self.op("dve", lambda: nc.vector.memset(bdp.ap, 0.0), writes=[bdp])
        sv0 = self.s2n_all.ap[0:64, :].rearrange("p (pr m) -> p pr m", pr=NPAIR)
        dv0 = bdp.ap[0:64, :].rearrange("p (pr c) -> p pr c", pr=NPAIR)[
            :, :, 0:M]
        sv1 = self.s2n_all.ap[64:128, :].rearrange("p (pr m) -> p pr m",
                                                   pr=NPAIR)
        dv1 = bdp.ap[64:128, :].rearrange("p (pr c) -> p pr c", pr=NPAIR)[
            :, :, M:128]
        self.op("dve", lambda: nc.vector.tensor_copy(dv0, sv0),
                reads=[self.s2n_all, bdp], writes=[bdp])
        self.op("dve", lambda: nc.vector.tensor_copy(dv1, sv1),
                reads=[self.s2n_all, bdp], writes=[bdp])
        for g in range(2):
            pst = self.pb[g]
            emits = []
            for j in range(4):
                pr = g * 4 + j
                emits.append(lambda j=j, pr=pr, pst=pst: nc.tensor.transpose(
                    pst.ap[:, j * 128:(j + 1) * 128],
                    bdp.ap[:, pr * 128:(pr + 1) * 128], self.idf_sb.ap))
            self.op("pe", emits, reads=[bdp, self.idf_sb], writes=[pst])
            self.op("dve", lambda g=g, pst=pst: nc.vector.tensor_copy(
                self.s2bd_all.ap[:, g * 512:(g + 1) * 512], pst.ap),
                reads=[pst], writes=[self.s2bd_all])
        bdv = self.s2bd_all.ap
        self.op("dve", lambda: nc.vector.tensor_copy(
            self.s2T_all.ap[0:64, :].rearrange("p (pr m) -> p pr m", pr=NPAIR),
            bdv[0:64, :].rearrange("p (pr c) -> p pr c", pr=NPAIR)[:, :, 0:M]),
            reads=[self.s2bd_all], writes=[self.s2T_all])
        self.op("dve", lambda: nc.vector.tensor_copy(
            self.s2T_all.ap[64:128, :].rearrange("p (pr m) -> p pr m",
                                                 pr=NPAIR),
            bdv[64:128, :].rearrange("p (pr c) -> p pr c", pr=NPAIR)[
                :, :, M:128]),
            reads=[self.s2T_all, self.s2bd_all], writes=[self.s2T_all])
        if ds2 is not None:
            self.dma(ds2.ap[s], self.s2n_all.ap, reads=(self.s2n_all,))
        if dO is not None:
            self.dma(dO.ap[s], self.O_all.ap, reads=(self.O_all,))

    def phase_B2(self, s):
        nc = self.nc
        W = NPAIR * M
        scr = self.h2
        z = Buf(scr.ap[:, 0:W], "z")
        zT = Buf(scr.ap[:, 512:512 + W], "zT")
        xz = Buf(scr.ap[:, 1024:1024 + W], "xz")
        xzT = Buf(scr.ap[:, 1536:1536 + W], "xzT")
        t1 = Buf(scr.ap[:, 2048:2048 + W], "t1")
        t5 = Buf(scr.ap[:, 2560:2560 + W], "t5")
        s2bd = self.s2bd_all
        xzbd = Buf(scr.ap[:, 4096:4096 + 1024], "xzbd")
        aux = Buf(scr.ap[:, 5120:5120 + 1024], "auxbd")
        zTbd = Buf(scr.ap[:, 3072:3072 + 1024], "zTbd")
        rctile = Buf(scr.ap[:, 6144:6144 + W], "rctile")
        for b in (xzbd, aux, zTbd):
            self.op("dve", lambda b=b: nc.vector.memset(b.ap, 0.0), writes=[b])

        def bdify(dst, src):
            sv0 = src.ap[0:64, :].rearrange("p (pr m) -> p pr m", pr=NPAIR)
            dv0 = dst.ap[0:64, :].rearrange("p (pr c) -> p pr c", pr=NPAIR)[
                :, :, 0:M]
            sv1 = src.ap[64:128, :].rearrange("p (pr m) -> p pr m", pr=NPAIR)
            dv1 = dst.ap[64:128, :].rearrange("p (pr c) -> p pr c", pr=NPAIR)[
                :, :, M:128]
            self.op("dve", lambda: nc.vector.tensor_copy(dv0, sv0),
                    reads=[src, dst], writes=[dst])
            self.op("dve", lambda: nc.vector.tensor_copy(dv1, sv1),
                    reads=[src, dst], writes=[dst])

        i7b = self.i7.ap[:, None, :].broadcast_to([128, NPAIR, M])
        i15b = self.i15.ap[:, None, :].broadcast_to([128, NPAIR, M])
        i13b = self.i13.ap[:, None, :].broadcast_to([128, NPAIR, M])

        def v3(b):
            return b.ap.rearrange("p (pr m) -> p pr m", pr=NPAIR)

        psc = self.pb[0]
        self.op("pe", lambda: nc.tensor.matmul(
            psc.ap[:2, 0:W], self.masks.ap, self.s2n_all.ap, start=True,
            stop=True), reads=[self.masks, self.s2n_all], writes=[psc])
        csum = Buf(self.sml.ap[:2, 108:108 + W], "csum")
        self.op("dve", lambda: nc.vector.tensor_copy(csum.ap, psc.ap[:2, 0:W]),
                reads=[psc], writes=[csum])
        cmax = Buf(self.sml.ap[:2, 620:628], "cmax")
        self.op("dve", lambda: nc.vector.reduce_max(
            cmax.ap, csum.ap.rearrange("p (pr m) -> p pr m", pr=NPAIR),
            axis=AX.X), reads=[csum], writes=[cmax])
        psr = self.pb[1]
        self.op("pe", lambda: nc.tensor.matmul(
            psr.ap[:2, 0:W], self.masks.ap, self.s2T_all.ap, start=True,
            stop=True), reads=[self.masks, self.s2T_all], writes=[psr])
        rsum = Buf(self.sml.ap[:2, 640:640 + W], "rsum")
        self.op("dve", lambda: nc.vector.tensor_copy(rsum.ap, psr.ap[:2, 0:W]),
                reads=[psr], writes=[rsum])
        rmax = Buf(self.sml.ap[:2, 1160:1168], "rmax")
        self.op("dve", lambda: nc.vector.reduce_max(
            rmax.ap, rsum.ap.rearrange("p (pr m) -> p pr m", pr=NPAIR),
            axis=AX.X), reads=[rsum], writes=[rmax])
        den = Buf(self.sml.ap[:2, 1170:1178], "den")
        self.op("dve", lambda: nc.vector.tensor_tensor(
            den.ap, cmax.ap, rmax.ap, ALU.mult), reads=[cmax, rmax],
            writes=[den])
        rden = Buf(self.sml.ap[:2, 1180:1188], "rden")
        self.op("dve", lambda: nc.vector.reciprocal(rden.ap, den.ap),
                reads=[den], writes=[rden])
        rdex = Buf(self.sml.ap[:2, 1190:1190 + W], "rdex")
        self.op("dve", lambda: nc.vector.tensor_copy(
            rdex.ap.rearrange("p (pr m) -> p pr m", pr=NPAIR),
            rden.ap[:, :, None].broadcast_to([2, NPAIR, M])),
            reads=[rden], writes=[rdex])
        psb2 = self.pb[2]
        self.op("pe", lambda: nc.tensor.matmul(
            psb2.ap[:, 0:W], self.masksT.ap, rdex.ap, start=True, stop=True),
            reads=[self.masksT, rdex], writes=[psb2])
        self.op("dve", lambda: nc.vector.tensor_copy(rctile.ap, psb2.ap[:, 0:W]),
                reads=[psb2], writes=[rctile])
        self.op("dve", lambda: nc.vector.tensor_tensor(
            z.ap, self.s2T_all.ap, rctile.ap, ALU.mult),
            reads=[self.s2T_all, rctile], writes=[z])
        self.op("dve", lambda: nc.vector.tensor_tensor(
            zT.ap, self.s2n_all.ap, rctile.ap, ALU.mult),
            reads=[self.s2n_all, rctile], writes=[zT])

        def pair_mms(psd, lhsbd, rhs):
            emits = []
            for pr in range(NPAIR):
                emits.append(lambda pr=pr: nc.tensor.matmul(
                    psd.ap[:, pr * M:(pr + 1) * M],
                    lhsbd.ap[:, pr * 128:(pr + 1) * 128],
                    rhs.ap[:, pr * M:(pr + 1) * M], start=True, stop=True))
            return emits

        for it in range(PINV_ITERS):
            bdify(zTbd, zT)
            psx = self.pb[3]
            self.op("pe", pair_mms(psx, s2bd, z), reads=[s2bd, z], writes=[psx])
            bdify(aux, psx)
            for g in range(2):
                psxt = self.pb[4 + g]
                emits = []
                for j in range(4):
                    pr = g * 4 + j
                    emits.append(lambda j=j, pr=pr, psxt=psxt:
                                 nc.tensor.transpose(
                                     psxt.ap[:, j * 128:(j + 1) * 128],
                                     aux.ap[:, pr * 128:(pr + 1) * 128],
                                     self.idf_sb.ap))
                self.op("pe", emits, reads=[aux, self.idf_sb], writes=[psxt])
                self.op("dve", lambda g=g, psxt=psxt: nc.vector.tensor_copy(
                    xzbd.ap[:, g * 512:(g + 1) * 512], psxt.ap),
                    reads=[psxt], writes=[xzbd])
            self.op("dve", lambda: nc.vector.scalar_tensor_tensor(
                v3(t1), psx.ap.rearrange("p (pr m) -> p pr m", pr=NPAIR),
                -1.0, i7b, ALU.mult, ALU.add),
                reads=[psx, self.i7], writes=[t1])
            pst2 = self.pb[5]
            self.op("pe", pair_mms(pst2, xzbd, t1), reads=[xzbd, t1],
                    writes=[pst2])
            self.op("dve", lambda: nc.vector.scalar_tensor_tensor(
                v3(t1), pst2.ap.rearrange("p (pr m) -> p pr m", pr=NPAIR),
                -1.0, i15b, ALU.mult, ALU.add),
                reads=[pst2, self.i15], writes=[t1])
            pst4 = self.pb[6]
            self.op("pe", pair_mms(pst4, xzbd, t1), reads=[xzbd, t1],
                    writes=[pst4])
            self.op("dve", lambda: nc.vector.scalar_tensor_tensor(
                v3(t5), pst4.ap.rearrange("p (pr m) -> p pr m", pr=NPAIR),
                -1.0, i13b, ALU.mult, ALU.add),
                reads=[pst4, self.i13], writes=[t5])
            bdify(aux, t5)
            psz = self.pb[3]
            self.op("pe", pair_mms(psz, zTbd, t5), reads=[zTbd, t5],
                    writes=[psz])
            pszT = self.pb[5]
            self.op("pe", pair_mms(pszT, aux, zT), reads=[aux, zT],
                    writes=[pszT])
            self.op("dve", lambda: nc.vector.tensor_scalar(
                z.ap, psz.ap, 0.25, None, ALU.mult), reads=[psz], writes=[z])
            self.op("dve", lambda: nc.vector.tensor_scalar(
                zT.ap, pszT.ap, 0.25, None, ALU.mult), reads=[pszT],
                writes=[zT])
        bdify(zTbd, zT)
        pskv = self.pb[0]
        self.op("pe", pair_mms(pskv, zTbd, self.O_all),
                reads=[zTbd, self.O_all], writes=[pskv])
        zkv_region = self.sml.ap[:, 100:1124]
        self.zkv_all = Buf(self.sml2.ap[:, 0:256].bitcast(BF16), "zkv_all")
        self.zkv_all = Buf(self.sml.ap[:, 1280:1536].bitcast(BF16), "zkv_all")
        self.op("dve", lambda: nc.vector.tensor_copy(
            self.zkv_all.ap, pskv.ap), reads=[pskv], writes=[self.zkv_all])
        dz = self.dbg_out("z_all", [self.nseq, 128, 512]) if s == 0 else \
            self.dbg_bufs.get("z_all")
        if dz is not None:
            self.dma(dz.ap[s], z.ap, reads=(z,))

    def phase_B3(self, s):
        nc = self.nc
        L, NB = self.L, self.NB
        h2bf = self.h2.ap.bitcast(BF16)
        q_all = Buf(h2bf[:, 0:NPAIR * L], "q_all")
        amix = Buf(self.mix.ap[:, 0:NPAIR * L], "amix")
        scrf = self.scr
        klbd = Buf(scrf.ap[:, 512:1024].bitcast(BF16), "klbd")
        zkbd = Buf(scrf.ap[:, 1024:1536].bitcast(BF16), "zkbd")
        for (dst, src) in ((klbd, self.kl_all), (zkbd, self.zkv_all)):
            self.op("dve", lambda dst=dst: nc.vector.memset(dst.ap, 0.0),
                    writes=[dst])
            sv0 = src.ap[0:64, :].rearrange("p (pr m) -> p pr m", pr=NPAIR)
            dv0 = dst.ap[0:64, :].rearrange("p (pr c) -> p pr c", pr=NPAIR)[
                :, :, 0:M]
            sv1 = src.ap[64:128, :].rearrange("p (pr m) -> p pr m", pr=NPAIR)
            dv1 = dst.ap[64:128, :].rearrange("p (pr c) -> p pr c", pr=NPAIR)[
                :, :, M:128]
            self.op("dve", lambda dv0=dv0, sv0=sv0: nc.vector.tensor_copy(
                dv0, sv0), reads=[src, dst], writes=[dst])
            self.op("dve", lambda dv1=dv1, sv1=sv1: nc.vector.tensor_copy(
                dv1, sv1), reads=[src, dst], writes=[dst])
        p1 = [Buf(scrf.ap[:, i * 512:(i + 1) * 512].bitcast(BF16), "p1_%d" % i)
              for i in (3, 4)]
        s1t = [Buf(scrf.ap[:, i * 512:(i + 1) * 512].bitcast(BF16), "s1t%d" % i)
               for i in (5, 6)]
        rmax1 = [Buf(self.sml.ap[:, 84 + i:85 + i], "rm%d" % i) for i in (0, 1)]
        nmax1 = [Buf(self.sml.ap[:, 86 + i:87 + i], "nm%d" % i) for i in (0, 1)]
        rm_bl = [Buf(self.sml.ap[:, 88 + i:89 + i], "rmb%d" % i) for i in (0, 1)]
        nm_bl = [Buf(self.sml.ap[:, 90 + i:91 + i], "nmb%d" % i) for i in (0, 1)]
        ssum = [Buf(self.sml.ap[:, 92 + 16 * i:108 + 16 * i], "ss%d" % i)
                for i in (0, 1)]
        srec = [Buf(self.sml.ap[:, 124 + 16 * i:140 + 16 * i], "sr%d" % i)
                for i in (0, 1)]
        damix = self.dbg_out("amix", [self.nseq, NPAIR, 128, L]) if s == 0 \
            else self.dbg_bufs.get("amix")

        for pr in range(NPAIR):
            self.dma(Buf(q_all.ap[:, pr * L:(pr + 1) * L], "qld%d" % pr),
                     self.q_sp[pr], writes=(q_all,),
                     q="act" if pr % 2 else "sp")
        for tb in range(NB):
            par = tb % 2
            psl = self.pb[0 + par]
            psla = self.pb[0 + 2 * par]
            pslb = self.pb[1 + 2 * par]
            emits_a, emits_b = [], []
            for pr in range(NPAIR):
                tgt = psla if pr < 4 else pslb
                lst = emits_a if pr < 4 else emits_b
                lst.append(lambda pr=pr, tb=tb, tgt=tgt: nc.tensor.matmul(
                    tgt.ap[:, (pr % 4) * 128:(pr % 4) * 128 + 128],
                    q_all.ap[:, pr * L + tb * 128:pr * L + tb * 128 + 128],
                    klbd.ap[:, pr * 128:(pr + 1) * 128],
                    start=True, stop=True))
            self.op("pe", emits_a, reads=[q_all, klbd], writes=[psla])
            self.op("pe", emits_b, reads=[q_all, klbd], writes=[pslb])
            self.op("dve", lambda par=par, psla=psla: nc.vector.reduce_max(
                rmax1[par].ap, psla.ap, axis=AX.X),
                reads=[psla], writes=[rmax1[par]])
            self.op("dve", lambda par=par, pslb=pslb: nc.vector.tensor_scalar(
                nmax1[par].ap, rmax1[par].ap, -1.0, None, ALU.mult),
                reads=[rmax1[par]], writes=[nmax1[par]])
            rm_b = rm_bl[par]
            nm_b = nm_bl[par]
            self.op("dve", lambda pslb=pslb, rm_b=rm_b: nc.vector.reduce_max(
                rm_b.ap, pslb.ap, axis=AX.X), reads=[pslb], writes=[rm_b])
            self.op("dve", lambda rm_b=rm_b, nm_b=nm_b: nc.vector.tensor_scalar(
                nm_b.ap, rm_b.ap, -1.0, None, ALU.mult),
                reads=[rm_b], writes=[nm_b])
            self.op("act", lambda par=par, psla=psla: nc.scalar.activation(
                p1[par].ap[:, 0:512], psla.ap, AF.Exp, bias=nmax1[par].ap),
                reads=[psla, nmax1[par]], writes=[p1[par]])
            self.op("act", lambda par=par, pslb=pslb, nm_b=nm_b:
                    nc.scalar.activation(
                        p1[par].ap[:, 512:1024], pslb.ap, AF.Exp,
                        bias=nm_b.ap),
                    reads=[pslb, nm_b], writes=[p1[par]])
            self.op("dve", lambda par=par: nc.vector.reduce_sum(
                ssum[par].ap, p1[par].ap.rearrange(
                    "p (g m) -> p g m", m=M), axis=AX.X),
                reads=[p1[par]], writes=[ssum[par]])
            self.op("dve", lambda par=par: nc.vector.reciprocal(
                srec[par].ap, ssum[par].ap),
                reads=[ssum[par]], writes=[srec[par]])
            self.op("dve", lambda par=par: nc.vector.tensor_tensor(
                p1[par].ap.rearrange("p (g m) -> p g m", m=M),
                p1[par].ap.rearrange("p (g m) -> p g m", m=M),
                srec[par].ap[:, :, None].broadcast_to([128, 16, M]),
                ALU.mult), reads=[p1[par], srec[par]], writes=[p1[par]])
            pstr = self.pb[4 + par]
            pbf = pstr.ap.bitcast(BF16)
            emits = []
            for j in range(4):
                emits.append(lambda j=j, par=par, pbf=pbf: nc.tensor.transpose(
                    pbf[:, j * 128:(j + 1) * 128],
                    p1[par].ap[:, j * 128:(j + 1) * 128], self.idb_sb.ap))
            self.op("pe", emits, reads=[p1[par], self.idb_sb], writes=[pstr])
            self.op("dve", lambda par=par, pbf=pbf: nc.vector.tensor_copy(
                s1t[par].ap[:, 0:512], pbf[:, 0:512]),
                reads=[pstr], writes=[s1t[par]])
            emits = []
            for j in range(4, 8):
                emits.append(lambda j=j, par=par, pbf=pbf: nc.tensor.transpose(
                    pbf[:, (j - 4) * 128:(j - 3) * 128],
                    p1[par].ap[:, j * 128:(j + 1) * 128], self.idb_sb.ap))
            self.op("pe", emits, reads=[p1[par], self.idb_sb], writes=[pstr])
            self.op("dve", lambda par=par, pbf=pbf: nc.vector.tensor_copy(
                s1t[par].ap[:, 512:1024], pbf[:, 0:512]),
                reads=[pstr], writes=[s1t[par]])
            psm = self.pb[6 + par]
            emits = []
            for pr in range(NPAIR):
                emits.append(lambda pr=pr, par=par, psm=psm: nc.tensor.matmul(
                    psm.ap[:, (pr % 4) * 128:(pr % 4) * 128 + 128]
                    if pr < 4 else
                    psm.ap[:, (pr % 4) * 128:(pr % 4) * 128 + 128],
                    zkbd.ap[:, pr * 128:(pr + 1) * 128],
                    s1t[par].ap[:, pr * 128:pr * 128 + 128],
                    start=True, stop=True))
            self.op("pe", emits[:4], reads=[zkbd, s1t[par]], writes=[psm])
            for j4 in range(4):
                self.op("act", lambda j4=j4, tb=tb, psm=psm: nc.scalar.copy(
                    amix.ap[:, j4 * L + tb * 128:j4 * L + tb * 128 + 128],
                    psm.ap[:, j4 * 128:j4 * 128 + 128]),
                    reads=[psm], writes=[amix])
            self.op("pe", emits[4:], reads=[zkbd, s1t[par]], writes=[psm])
            for j4 in range(4):
                self.op("act", lambda j4=j4, tb=tb, psm=psm: nc.scalar.copy(
                    amix.ap[:, (4 + j4) * L + tb * 128:
                            (4 + j4) * L + tb * 128 + 128],
                    psm.ap[:, j4 * 128:j4 * 128 + 128]),
                    reads=[psm], writes=[amix])
        wocb = [Buf(h2bf[:, i * 1024:(i + 1) * 1024], "wocb%d" % i)
                for i in (0, 1)]
        bo_bc3 = Buf(self.scr.ap[:, 3584:4096].bitcast(BF16), "bo_bc3")
        self.dma(bo_bc3, self.bo_bf_sp.broadcast_to([128, D]))
        attn_st = [Buf(self.scr.ap[:, 1536 + i * 512:2048 + i * 512]
                       .bitcast(BF16), "attn_st%d" % i) for i in (0, 1)]
        for g4 in range((NB + 3) // 4):
            gn = min(4, NB - g4 * 4)
            for kc in range(KC):
                wo_b = wocb[kc % 2]
                self.dma(wo_b, self.wo_sp[kc], q="act",
                         reads=(q_all,) if (g4 == 0 and kc < 2) else ())
                for j in range(gn):
                    tb = g4 * 4 + j
                    for half in range(2):
                        ps_w = self.pb[2 * j + half]
                        self.op("pe", lambda kc=kc, half=half, tb=tb, j=j,
                                ps_w=ps_w, wo_b=wo_b: nc.tensor.matmul(
                                    ps_w.ap,
                                    amix.ap.rearrange(
                                        "p (pr t) -> p pr t", pr=NPAIR)[
                                        :, kc, tb * 128:tb * 128 + 128],
                                    wo_b.ap[:, half * 512:half * 512 + 512],
                                    start=(kc == 0), stop=(kc == KC - 1)),
                                reads=[amix, wo_b], writes=[ps_w])
            for j in range(gn):
                tb = g4 * 4 + j
                ast = attn_st[j % 2]
                for half in range(2):
                    self.op("dve", lambda j=j, half=half, ast=ast:
                            nc.vector.tensor_tensor(
                                ast.ap[:, half * 512:half * 512 + 512],
                                self.pb[2 * j + half].ap,
                                bo_bc3.ap[:, half * 512:half * 512 + 512],
                                ALU.add),
                            reads=[self.pb[2 * j + half], bo_bc3],
                            writes=[ast])
                self.dma(self.attn_sp[s, tb], ast.ap, reads=(ast,))
        if damix is not None:
            af32 = Buf(self.scr.ap[:, 3584 - 2048 + 2048:][:, 0:0], "x")
            af32 = Buf(self.h2.ap[:, 8192:8192 + L], "amixdbg")
            for pr in range(NPAIR):
                self.op("dve", lambda pr=pr: nc.vector.tensor_copy(
                    af32.ap, amix.ap[:, pr * L:(pr + 1) * L]),
                    reads=[amix], writes=[af32])
                self.dma(damix.ap[s, pr], af32.ap, reads=(af32,))

    def phase_P4(self, s):
        nc = self.nc
        L, NB, NT, TT = self.L, self.NB, self.NT, self.TT
        h2 = self.h2
        h2q = h2.ap.bitcast(BF16)
        mix = self.mix
        scr = self.scr
        xqsw = [Buf(mix.ap[:, 0:KC * TT], "xqsw")]
        amixw = [Buf(mix.ap[:, 4096:8192], "amixw0"),
                 Buf(mix.ap[:, 12288:16384], "amixw1")]
        xo_zone = mix.ap[:, 8192:10240].bitcast(F32)
        ostage = Buf(xo_zone[:, 0:1024], "ostage")
        w2tb = [Buf(mix.ap[:, 10240 + 1024 * i:11264 + 1024 * i],
                    "w2tb%d" % i) for i in (0, 1)]
        hbuf = [Buf(scr.ap[:, i * 514:i * 514 + 514], "h%d" % i) for i in (0, 1)]
        accA = [Buf(scr.ap[:, 1028 + i * 512:1540 + i * 512], "accA%d" % i)
                for i in (0, 1)]
        accB = [Buf(scr.ap[:, 2052 + i * 512:2564 + i * 512], "accB%d" % i)
                for i in (0, 1)]
        macc = Buf(scr.ap[:, 3076:3588], "macc")
        tq = Buf(scr.ap[:, 3588:4100], "tq")
        mT = tq
        absb2 = [Buf(self.sml.ap[:, 504:1016], "absb0"), tq]
        s2bc = hbuf[0]
        seam_all = Buf(self.sml2.ap[:, 256:256 + max(1, NHC * 2 * (NT - 1))],
                       "seam_all")
        mrow = Buf(self.sml2.ap[:1, 448:448 + TT], "mrow")
        cmax = Buf(self.sml.ap[:, 90:94], "cmaxq")
        cl2 = Buf(self.sml.ap[:, 94:98], "cl2")
        r2 = Buf(self.sml.ap[:, 98:102], "r2q")
        s127 = Buf(self.sml.ap[:, 102:106], "s127")
        iv2 = Buf(self.sml.ap[:, 106:110], "iv2")
        sc2 = Buf(self.sml.ap[:, 110:114], "sc2")
        nseam = NT - 1
        NTB = TT // 128
        h2slot = [Buf(h2.ap[:, i * 512:(i + 1) * 512], "h2s%d" % i)
                  for i in range(NHC)]
        dffn = self.dbg_out("h2dbg", [self.nseq, NT, 128, NHC * TT // 4]) \
            if False else None

        if nseam > 0:
            xqseam = Buf(self.sml.ap[:, 480:480 + KC * nseam].bitcast(BF16),
                         "xqseam")
            invseam = Buf(self.sml.ap[:, 114:114 + 2 * nseam], "invseam")
            for k in range(1, NT):
                dst = xqseam.ap.rearrange(
                    "p (kc n c) -> p kc n c", kc=KC, n=nseam)[:, :, k - 1, :]
                self.dma(Buf(dst, "xqs_k"),
                         self.xqs_sp[s][:, :, k * 512 - 1:k * 512 + 1]
                         .rearrange("kc p c -> p kc c"),
                         writes=(xqseam,))
                self.op("dve", lambda k=k: nc.vector.tensor_copy(
                    invseam.ap[:, (k - 1) * 2:(k - 1) * 2 + 2],
                    self.invbc_b.ap[:, k * 512 - 1:k * 512 + 1]),
                    reads=[self.invbc_b], writes=[invseam])
            for hc in range(NHC):
                pss = self.pb[6 + hc % 2]
                emits = []
                for kc in range(KC):
                    emits.append(lambda kc=kc, hc=hc, pss=pss:
                                 nc.tensor.matmul(
                                     pss.ap[:, 0:2 * nseam],
                                     self.w1t.ap[:, kc * DH + hc * 128:
                                                 kc * DH + hc * 128 + 128],
                                     xqseam.ap[:, kc * 2 * nseam:
                                               (kc + 1) * 2 * nseam],
                                     start=(kc == 0), stop=(kc == KC - 1)))
                self.op("pe", emits, reads=[self.w1t, xqseam], writes=[pss])
                self.op("dve", lambda hc=hc, pss=pss: nc.vector.tensor_tensor(
                    seam_all.ap[:, hc * 2 * nseam:(hc + 1) * 2 * nseam],
                    pss.ap[:, 0:2 * nseam], invseam.ap, ALU.mult),
                    reads=[pss, invseam], writes=[seam_all])

        s2r_tok = [None]
        asm_q = []
        for t in range(NT):
            t0 = t * TT
            xq_b = xqsw[0]
            am_b = amixw[t % 2]
            self.dma(Buf(xq_b.ap.rearrange("p (kc c) -> p kc c", kc=KC),
                         "xqbv"),
                     self.xqs_sp[s][:, :, t0:t0 + TT].rearrange(
                         "kc p c -> p kc c"), writes=(xq_b,))
            self.dma(am_b.ap[:, 0:NTB * D].rearrange("p (tb d) -> p tb d",
                                                     tb=NTB),
                     self.attn_sp[s][t * NTB:(t + 1) * NTB].rearrange(
                         "tb p d -> p tb d"), writes=(am_b,), q="act")
            pend_max = [None]

            def stage_a(hc):
                par = hc % 2
                psm = self.pb[par]
                hcur = hbuf[par]
                emits = []
                for kc in range(KC):
                    emits.append(lambda kc=kc, hc=hc, psm=psm:
                                 nc.tensor.matmul(
                                     psm.ap[:, 0:TT],
                                     self.w1t.ap[:, kc * DH + hc * 128:
                                                 kc * DH + hc * 128 + 128],
                                     xq_b.ap[:, kc * TT:(kc + 1) * TT],
                                     start=(kc == 0), stop=(kc == KC - 1)))
                self.op("pe", emits, reads=[self.w1t, xq_b], writes=[psm])
                self.op("dve", lambda psm=psm, hcur=hcur:
                        nc.vector.tensor_tensor(
                            hcur.ap[:, 1:TT + 1], psm.ap[:, 0:TT],
                            self.invbc_b.ap[:, t0:t0 + TT], ALU.mult),
                        reads=[psm, self.invbc_b], writes=[hcur])
                if t == 0:
                    self.op("gp", lambda hcur=hcur: nc.gpsimd.memset(
                        hcur.ap[:, 0:1], 0.0), reads=[], writes=[hcur])
                if t == NT - 1:
                    self.op("gp", lambda hcur=hcur: nc.gpsimd.memset(
                        hcur.ap[:, TT + 1:TT + 2], 0.0), reads=[],
                        writes=[hcur])
                if nseam > 0:
                    if 0 < t < NT - 1:
                        sl = hc * 2 * nseam + (t - 1) * 2
                        sv = seam_all.ap[:, sl:sl + 4:3]
                        dv = hcur.ap[:, 0:TT + 2:TT + 1]
                        self.op("dve", lambda sv=sv, dv=dv:
                                nc.vector.tensor_copy(dv, sv),
                                reads=[seam_all], writes=[hcur])
                    elif 0 < t:
                        sl = hc * 2 * nseam + (t - 1) * 2
                        self.op("dve", lambda hcur=hcur, sl=sl:
                                nc.vector.tensor_copy(
                                    hcur.ap[:, 0:1],
                                    seam_all.ap[:, sl:sl + 1]),
                                reads=[seam_all], writes=[hcur])
                    elif t < NT - 1:
                        sr = hc * 2 * nseam + t * 2 + 1
                        self.op("dve", lambda hcur=hcur, sr=sr:
                                nc.vector.tensor_copy(
                                    hcur.ap[:, TT + 1:TT + 2],
                                    seam_all.ap[:, sr:sr + 1]),
                                reads=[seam_all], writes=[hcur])
                acc1 = accA[par]
                self.op("act", lambda hcur=hcur, hc=hc, acc1=acc1:
                        nc.scalar.activation(
                            acc1.ap[:, 0:TT], hcur.ap[:, 1:TT + 1],
                            AF.Identity,
                            bias=self.cb.ap[:, hc:hc + 1],
                            scale=self.cw2.ap[:, 3 * hc + 1:3 * hc + 2]),
                        reads=[hcur, self.cw2, self.cb], writes=[acc1])

            def stage_b(hc):
                par = hc % 2
                hcur = hbuf[par]
                acc1, acc2 = accA[par], accB[par]
                acc3 = acc1
                self.op("dve", lambda hcur=hcur, hc=hc, acc1=acc1, acc2=acc2:
                        nc.vector.scalar_tensor_tensor(
                            acc2.ap[:, 0:TT], hcur.ap[:, 0:TT],
                            self.cw2.ap[:, 3 * hc:3 * hc + 1],
                            acc1.ap[:, 0:TT], ALU.mult, ALU.add),
                        reads=[hcur, acc1, self.cw2], writes=[acc2])
                self.op("dve", lambda hcur=hcur, hc=hc, acc2=acc2, acc3=acc3:
                        nc.vector.scalar_tensor_tensor(
                            acc3.ap[:, 0:TT], hcur.ap[:, 2:TT + 2],
                            self.cw2.ap[:, 3 * hc + 2:3 * hc + 3],
                            acc2.ap[:, 0:TT], ALU.mult, ALU.add),
                        reads=[hcur, acc2, self.cw2], writes=[acc3])
                self.op("act", lambda hc=hc, acc3=acc3: nc.scalar.activation(
                    h2.ap[:, hc * 512:hc * 512 + TT], acc3.ap[:, 0:TT],
                    AF.Silu), reads=[acc3], writes=[h2slot[hc]])
                if hc == 0:
                    self.op("act", lambda hc=hc: nc.scalar.activation(
                        macc.ap[:, 0:TT], h2.ap[:, hc * 512:hc * 512 + TT],
                        AF.Abs), reads=[h2slot[hc]], writes=[macc])
                else:
                    absb = absb2[par]
                    aap = absb.ap[:, 0:TT]
                    self.op("act", lambda hc=hc, aap=aap: nc.scalar.activation(
                        aap, h2.ap[:, hc * 512:hc * 512 + TT],
                        AF.Abs), reads=[h2slot[hc]], writes=[absb])
                    if pend_max[0] is not None:
                        pend_max[0]()
                    pend_max[0] = (lambda aap=aap, absb=absb: self.op(
                        "dve", lambda: nc.vector.tensor_tensor(
                            macc.ap[:, 0:TT], macc.ap[:, 0:TT], aap,
                            ALU.max), reads=[absb, macc], writes=[macc]))

            stage_a(0)
            for hc in range(NHC):
                if hc + 1 < NHC:
                    stage_a(hc + 1)
                stage_b(hc)
                if asm_q and hc % 4 == 3:
                    asm_q.pop(0)()
            if pend_max[0] is not None:
                pend_max[0]()
                pend_max[0] = None
            psq = self.pb[2]
            emits = []
            for j in range(NTB):
                emits.append(lambda j=j, psq=psq: nc.tensor.transpose(
                    psq.ap[:, j * 128:(j + 1) * 128],
                    macc.ap[:, j * 128:(j + 1) * 128], self.idf_sb.ap))
            self.op("pe", emits, reads=[macc, self.idf_sb], writes=[psq])
            self.op("dve", lambda: nc.vector.tensor_copy(
                mT.ap[:, 0:TT], psq.ap[:, 0:TT]), reads=[psq], writes=[mT])
            self.op("dve", lambda: nc.vector.reduce_max(
                cmax.ap[:, 0:NTB], mT.ap[:, 0:TT].rearrange(
                    "p (j c) -> p j c", j=NTB), axis=AX.X,
                apply_absolute_value=True),
                reads=[mT], writes=[cmax])
            self.op("dve", lambda: nc.vector.tensor_scalar(
                cl2.ap[:, 0:NTB], cmax.ap[:, 0:NTB], 1e-5, None, ALU.max),
                reads=[cmax], writes=[cl2])
            self.op("dve", lambda: nc.vector.reciprocal(
                r2.ap[:, 0:NTB], cl2.ap[:, 0:NTB]), reads=[cl2], writes=[r2])
            self.op("dve", lambda: nc.vector.tensor_scalar(
                s127.ap[:, 0:NTB], r2.ap[:, 0:NTB], 127.0, None, ALU.mult),
                reads=[r2], writes=[s127])
            self.op("dve", lambda: nc.vector.tensor_scalar(
                iv2.ap[:, 0:NTB], cl2.ap[:, 0:NTB], 1.0 / 127.0, None,
                ALU.mult), reads=[cl2], writes=[iv2])
            self.op("dve", lambda: nc.vector.tensor_scalar(
                sc2.ap[:, 0:NTB], iv2.ap[:, 0:NTB], self.w2_mw.ap, None,
                ALU.mult), reads=[iv2, self.w2_mw], writes=[sc2])
            pst = self.pb[3]
            emits = []
            for j in range(NTB):
                emits.append(lambda j=j, pst=pst: nc.tensor.transpose(
                    pst.ap[:1, j * 128:j * 128 + 128], s127.ap[:, j:j + 1],
                    self.idf_sb.ap))
            self.op("pe", emits, reads=[s127, self.idf_sb], writes=[pst])
            self.op("dve", lambda: nc.vector.tensor_copy(
                mrow.ap, pst.ap[:1, 0:TT]), reads=[pst], writes=[mrow])
            tw = self.dma(self.s2r_sp, mrow.ap, reads=(mrow,),
                          wait=[s2r_tok[0]] if s2r_tok[0] else [])
            s2r_tok[0] = self.dma(Buf(s2bc.ap[:, 0:TT], "s2bcv"),
                                  self.s2r_sp.broadcast_to([128, TT]),
                                  writes=(s2bc,), wait=[tw])
            tqI = [Buf(tq.ap.bitcast(I16)[:, i * 512:(i + 1) * 512],
                       "tqI%d" % i) for i in (0, 1)]

            def q_stage(hc):
                par = hc % 2
                qa = accA[par]
                self.op("dve", lambda hc=hc, qa=qa: nc.vector.tensor_tensor(
                    qa.ap[:, 0:TT], h2.ap[:, hc * 512:hc * 512 + TT],
                    s2bc.ap[:, 0:TT], ALU.mult),
                    reads=[h2slot[hc], s2bc], writes=[qa])
                self.op("dve", lambda qa=qa, par=par: nc.vector.tensor_scalar(
                    tqI[par].ap[:, 0:TT], qa.ap[:, 0:TT], C_RND, C_RND,
                    ALU.add, ALU.subtract), reads=[qa], writes=[tqI[par]])
                self.op("act", lambda hc=hc, par=par: nc.scalar.copy(
                    h2q[:, hc * 1024:hc * 1024 + TT], tqI[par].ap[:, 0:TT]),
                    reads=[tqI[par]], writes=[h2slot[hc]])
                wb = w2tb[par]
                self.dma(wb, self.w2t_sp[hc, :, :],
                         q="sp" if par == 0 else "act")
                return wb

            def m_stage(hc, wb):
                for tb in range(NTB):
                    for half in range(2):
                        ps_o = self.pb[2 * tb + half]
                        self.op("pe", lambda hc=hc, half=half, tb=tb,
                                ps_o=ps_o, wb=wb: nc.tensor.matmul(
                                    ps_o.ap,
                                    h2q[:, hc * 1024 + tb * 128:
                                        hc * 1024 + tb * 128 + 128],
                                    wb.ap[:, half * 512:half * 512 + 512],
                                    start=(hc == 0), stop=(hc == NHC - 1)),
                                reads=[h2slot[hc], wb], writes=[ps_o])

            wb_prev = q_stage(0)
            for hc in range(NHC):
                wb_next = q_stage(hc + 1) if hc + 1 < NHC else None
                m_stage(hc, wb_prev)
                wb_prev = wb_next
            for tb in range(NTB):
                for half in range(2):
                    self.op("dve", lambda half=half, tb=tb:
                            nc.vector.tensor_scalar(
                                h2.ap[:, 12288 + tb * 1024 + half * 512:
                                      12288 + tb * 1024 + half * 512 + 512],
                                self.pb[2 * tb + half].ap,
                                sc2.ap[:, tb:tb + 1], None, ALU.mult),
                            reads=[self.pb[2 * tb + half], sc2],
                            writes=[h2slot[24 + 2 * tb + half]])
            def mk_asm(tb, t0=t0, am_b=am_b):
                def asm():
                    tok0 = t0 + tb * 128
                    xt0 = 8192 + tb * 1024
                    xsl = [h2slot[16 + 2 * tb], h2slot[17 + 2 * tb]]
                    self.dma(h2.ap[:, xt0:xt0 + 1024],
                             self.x_d[s, tok0:tok0 + 128, :], writes=xsl)
                    for half in range(2):
                        self.op("dve", lambda half=half, tb=tb, xt0=xt0:
                                nc.vector.tensor_tensor(
                                    ostage.ap[:, half * 512:half * 512 + 512],
                                    am_b.ap[:, tb * 1024 + half * 512:
                                            tb * 1024 + half * 512 + 512],
                                    h2.ap[:, xt0 + half * 512:
                                          xt0 + half * 512 + 512],
                                    ALU.add),
                                reads=[am_b, xsl[half]], writes=[ostage])
                        self.op("dve", lambda half=half, tb=tb:
                                nc.vector.tensor_tensor(
                                    ostage.ap[:, half * 512:half * 512 + 512],
                                    ostage.ap[:, half * 512:half * 512 + 512],
                                    h2.ap[:, 12288 + tb * 1024 + half * 512:
                                          12288 + tb * 1024 + half * 512 + 512],
                                    ALU.add),
                                reads=[ostage, h2slot[24 + 2 * tb + half]],
                                writes=[ostage])
                    self.dma(self.out_d[s, tok0:tok0 + 128, :], ostage.ap,
                             reads=(ostage,))
                return asm
            for tb in range(NTB):
                asm_q.append(mk_asm(tb))
        while asm_q:
            asm_q.pop(0)()


def _host_prep(inputs):
    x = np.ascontiguousarray(np.asarray(inputs["L0_attn_x"], dtype=np.float32))
    conv_w = np.asarray(inputs["L0_ffn_conv_w"], dtype=np.float32)
    conv_b = np.asarray(inputs["L0_ffn_conv_b"], dtype=np.float32)
    cw_pp = np.zeros((128, 3 * NHC), dtype=np.float32)
    cb_pp = np.zeros((128, NHC), dtype=np.float32)
    for hc in range(NHC):
        for j in range(3):
            cw_pp[:, 3 * hc + j] = conv_w[hc * 128:(hc + 1) * 128, 0, j]
        cb_pp[:, hc] = conv_b[hc * 128:(hc + 1) * 128]
    base = {
        "Wk": np.ascontiguousarray(np.asarray(inputs["L0_attn_Wk"], np.float32)),
        "Wq": np.ascontiguousarray(np.asarray(inputs["L0_attn_Wq"], np.float32)),
        "Wv": np.ascontiguousarray(np.asarray(inputs["L0_attn_Wv"], np.float32)),
        "Wo": np.ascontiguousarray(np.asarray(inputs["L0_attn_Wo"], np.float32)),
        "bo": np.ascontiguousarray(
            np.asarray(inputs["L0_attn_bo"], np.float32).reshape(1, D)),
        "w1": np.ascontiguousarray(np.asarray(inputs["L0_ffn_w1"], np.float32)),
        "w2": np.ascontiguousarray(np.asarray(inputs["L0_ffn_w2"], np.float32)),
        "convw_pp": cw_pp,
        "convb_pp": cb_pp,
    }
    return x, base


def kernel(**inputs):
    x, base = _host_prep(inputs)
    per = x.shape[0] // N_CORES
    b = Builder(L=x.shape[1], nseq=per)
    nc = b.build()
    in_maps = []
    for c in range(N_CORES):
        m = dict(base)
        m.update(b.hosts)
        m["x"] = np.ascontiguousarray(x[c * per:(c + 1) * per])
        in_maps.append(m)
    res = run_bass_kernel_spmd(nc, in_maps, core_ids=list(range(N_CORES)))
    out = np.concatenate([np.asarray(r["out"]) for r in res.results], axis=0)
    return out.astype(np.float32)
